# version 04
# baseline (speedup 1.0000x reference)
import math
import numpy as np
import ml_dtypes
from contextlib import ExitStack

import concourse.bass as bass
import concourse.mybir as mybir
from concourse.bass_utils import run_bass_kernel_spmd
from concourse.ap import AP
from bass_rust import IndirectOffsetOnAxis

F32 = mybir.dt.float32
BF16 = mybir.dt.bfloat16
U32 = mybir.dt.uint32
AF = mybir.ActivationFunctionType
ALU = mybir.AluOpType
AX = mybir.AxisListType

E = 1024
N = 2048
NH = 16
HD = 64
LH = 8
WIN = 512
HID = 2730
QK_SCALE = 8.0
NBUCK = 320
MAXDIST = 800
MASKVAL = -30000.0

NPT = 8
NTC = 4
TC = 512
ZW = 2052

DEBUG_STAGE = 0


def _bucket_of_rel():
    rel = np.arange(-2047, 2048)
    nb = NBUCK // 2
    buckets = (rel > 0).astype(np.int64) * nb
    rel_abs = np.abs(rel)
    max_exact = nb // 2
    is_small = rel_abs < max_exact
    large = np.log(np.maximum(rel_abs, 1).astype(np.float64) / max_exact) / \
        math.log(MAXDIST / max_exact) * (nb - max_exact)
    large = (max_exact + large).astype(np.int64)
    large = np.minimum(large, nb - 1)
    return buckets + np.where(is_small, rel_abs, large)

def make_consts():
    c = {}
    bk = _bucket_of_rel()
    S = np.zeros((NBUCK, 4095), np.float32)
    S[bk, np.arange(4095)] = 1.0
    c["c_S"] = S
    idx = np.zeros((NH, 128, 1), np.uint32)
    for h in range(NH):
        idx[h, :, 0] = h * 4224 + 127 - np.arange(128)
    c["c_stripidx"] = idx
    c["c_ident"] = np.eye(128, dtype=np.float32)
    q_ = np.arange(128)[:, None]
    k_ = np.arange(128)[None, :]
    c["c_mask0"] = np.where(k_ - q_ > 0, MASKVAL, 0.0).astype(np.float32)
    c["c_mask4"] = np.where(k_ - q_ <= 0, MASKVAL, 0.0).astype(np.float32)
    R = np.zeros((64, 64), np.float32)
    for do_ in range(32):
        R[do_ + 32, do_] = -1.0
    for do_ in range(32, 64):
        R[do_ - 32, do_] = 1.0
    R128 = np.zeros((128, 128), np.float32)
    R128[0:64, 0:64] = R
    R128[64:128, 64:128] = R
    c["c_rot"] = R128
    ones2 = np.zeros((128, 2), np.float32)
    ones2[0:64, 0] = 1.0
    ones2[64:128, 1] = 1.0
    c["c_ones2"] = ones2
    pat2 = np.zeros((2, 128), np.float32)
    pat2[0, 0:64] = 1.0
    pat2[1, 64:128] = 1.0
    c["c_pat2"] = pat2
    c["c_ones1"] = np.ones((1, 128), np.float32)
    c["c_onescol"] = np.ones((128, 1), np.float32)
    P8 = np.zeros((8, 2), np.float32)
    P8[0:4, 0] = 1.0
    P8[4:8, 1] = 1.0
    c["c_P8"] = P8
    hi = np.arange(0, HD, 2, dtype=np.float64)
    inv = 1.0 / 10000.0 ** (hi / HD)
    t = np.arange(N, dtype=np.float64)
    fr = np.concatenate([t[:, None] * inv[None, :]] * 2, axis=1)
    base = (hi + 0.4 * HD) / (1.4 * HD)
    power = (t - N // 2) / float(WIN // 2)
    sc = np.concatenate([base[None, :] ** power[:, None]] * 2, axis=1)
    dup = lambda a: np.concatenate([a, a], axis=0).astype(np.float32)
    c["c_cs"] = dup((np.cos(fr) * sc).T)
    c["c_ss"] = dup((np.sin(fr) * sc).T)
    c["c_ci"] = dup((np.cos(fr) / sc).T)
    c["c_si"] = dup((np.sin(fr) / sc).T)
    return c

def prep_host(inputs):
    g = inputs
    d = {}
    d["xin"] = np.ascontiguousarray(
        np.transpose(g["L0_attn_hidden_states"], (0, 2, 1))).reshape(16, 8, 128, N)
    for nm in ("Wq", "Wk", "Wo"):
        WT = g["L0_attn_" + nm].T
        d[nm + "slab"] = np.ascontiguousarray(
            WT.reshape(8, 128, 8, 128).transpose(2, 1, 0, 3).reshape(8, 128, 1024))
    d["WvTr"] = np.ascontiguousarray(
        g["L0_attn_Wv"].T.reshape(8, 128, 1024).transpose(1, 0, 2).reshape(128, 8192))
    d["WoT"] = np.ascontiguousarray(g["L0_attn_Wo"].T)
    for nm in ("bq", "bk", "bv", "bo"):
        d[nm] = np.ascontiguousarray(g["L0_attn_" + nm])
    d["gruW"] = np.ascontiguousarray(g["L0_attn_gru_W"])
    d["grub"] = np.ascontiguousarray(g["L0_attn_gru_b"].reshape(8, 1))
    d["gruc"] = np.ascontiguousarray(g["L0_attn_gru_const"].reshape(1, NH))
    d["rel"] = np.ascontiguousarray(g["L0_attn_rel_embed"])
    wqkv = g["L0_ffn_wqkv"]
    d["wqkv_slab"] = np.ascontiguousarray(
        wqkv.reshape(2, 8, 128, 12, 128).transpose(0, 3, 2, 1, 4).reshape(2, 12, 128, 1024)[:, 0:8])
    d["wqkv_vw"] = np.ascontiguousarray(
        wqkv.reshape(2, 8, 128, 1536)[:, :, :, 1024:].transpose(0, 2, 1, 3).reshape(2, 128, 4096))
    d["wo_slab"] = np.ascontiguousarray(
        g["L0_ffn_wo"].reshape(2, 4, 128, 8, 128).transpose(0, 3, 2, 1, 4).reshape(2, 8, 128, 512))
    w1a = np.zeros((2, 1024, 2816), np.float32); w1a[:, :, :HID] = g["L0_ffn_w1"][:, :, :HID]
    w1g = np.zeros((2, 1024, 2816), np.float32); w1g[:, :, :HID] = g["L0_ffn_w1"][:, :, HID:]
    d["w1a_slab"] = np.ascontiguousarray(
        w1a.reshape(2, 8, 128, 22, 128).transpose(0, 3, 2, 1, 4).reshape(2, 22, 128, 1024))
    d["w1g_slab"] = np.ascontiguousarray(
        w1g.reshape(2, 8, 128, 22, 128).transpose(0, 3, 2, 1, 4).reshape(2, 22, 128, 1024))
    b1ap = np.zeros((2, 2816), np.float32); b1ap[:, :HID] = g["L0_ffn_b1"][:, :HID]
    b1gp = np.zeros((2, 2816), np.float32); b1gp[:, :HID] = g["L0_ffn_b1"][:, HID:]
    d["b1a"] = b1ap
    d["b1g"] = b1gp
    w2p = np.zeros((2, 2816, 1024), np.float32); w2p[:, :HID] = g["L0_ffn_w2"]
    d["w2t"] = np.ascontiguousarray(
        w2p.reshape(2, 22, 128, 8, 128).transpose(0, 3, 2, 1, 4).reshape(2, 8, 128, 2816))
    d["b2"] = np.ascontiguousarray(g["L0_ffn_b2"])
    d["lng"] = np.ascontiguousarray(g["L0_ffn_ln_g"])
    d["lnb"] = np.ascontiguousarray(g["L0_ffn_ln_b"])
    qs = g["L0_ffn_q_scale"]; ks = g["L0_ffn_k_scale"]
    colpair = lambda v: np.ascontiguousarray(np.concatenate([v, v], axis=1)[:, :, None])
    d["qscol"] = colpair(qs)
    d["kscol"] = colpair(ks)
    d["qsrot"] = colpair(np.roll(qs, -32, axis=1))
    d["ksrot"] = colpair(np.roll(ks, -32, axis=1))
    d["qsrow"] = np.ascontiguousarray(qs.reshape(2, 1, 64))
    d["ksrow"] = np.ascontiguousarray(ks.reshape(2, 1, 64))
    d.update(make_consts())
    return d


class Sem:
    __slots__ = ("h", "n", "name")
    def __init__(self, nc, name):
        self.h = nc.alloc_semaphore(name=name)
        self.n = 0
        self.name = name

class Sched:
    ENGS = ("tensor", "vector", "scalar", "gpsimd", "sync")
    def __init__(self):
        self.q = {e: [] for e in self.ENGS}

    def op(self, eng, fn, waits=(), incs=()):
        ws = tuple((s, v) for (s, v) in waits if v > 0)
        for s, k in incs:
            s.n += k
        self.q[eng].append((ws, fn, tuple(incs)))
        if incs:
            s0 = incs[0][0]
            return (s0, s0.n)
        return None

    def emit(self, block):
        for ename in self.ENGS:
            items = self.q[ename]
            def make(items):
                def _(eng):
                    last = {}
                    for waits, fn, incs in items:
                        for s, v in waits:
                            if last.get(s.name, -1) < v:
                                eng.wait_ge(s.h, v)
                                last[s.name] = v
                        if fn is not None:
                            inst = fn(eng)
                            for s, k in incs:
                                inst.then_inc(s.h, k)
                return _
            getattr(block, ename)(make(items))


class BD:
    def __init__(self, nc, debug_stage=0):
        self.nc = nc
        self.sch = Sched()
        self.debug_stage = debug_stage
        self.final_waits = []
        self._mk_io()

    def _mk_io(self):
        nc = self.nc
        def inp(name, shape, dt=F32):
            return nc.dram_tensor(name, shape, dt, kind="ExternalInput").ap()
        self.xin = inp("xin", [2, 8, 128, N])
        self.Wslab = {nm: inp(nm + "slab", [8, 128, 1024]) for nm in ("Wq", "Wk", "Wo")}
        self.WvTr = inp("WvTr", [128, 8192])
        self.WoT = inp("WoT", [E, E])
        self.battn = {nm: inp(nm, [E]) for nm in ("bq", "bk", "bv", "bo")}
        self.gruW = inp("gruW", [8, 64])
        self.grub = inp("grub", [8, 1])
        self.gruc = inp("gruc", [1, NH])
        self.rel = inp("rel", [NBUCK, NH])
        self.wqkv_slab = inp("wqkv_slab", [2, 8, 128, 1024])
        self.wqkv_vw = inp("wqkv_vw", [2, 128, 4096])
        self.wo_slab = inp("wo_slab", [2, 8, 128, 512])
        self.w1a_slab = inp("w1a_slab", [2, 22, 128, 1024])
        self.w1g_slab = inp("w1g_slab", [2, 22, 128, 1024])
        self.b1a = inp("b1a", [2, 2816])
        self.b1g = inp("b1g", [2, 2816])
        self.w2t = inp("w2t", [2, 8, 128, 2816])
        self.b2 = inp("b2", [2, E])
        self.lng = inp("lng", [2, E])
        self.lnb = inp("lnb", [2, E])
        self.qscol = inp("qscol", [2, 128, 1]); self.kscol = inp("kscol", [2, 128, 1])
        self.qsrot = inp("qsrot", [2, 128, 1]); self.ksrot = inp("ksrot", [2, 128, 1])
        self.qsrow = inp("qsrow", [2, 1, 64]); self.ksrow = inp("ksrow", [2, 1, 64])
        self.c_S = inp("c_S", [NBUCK, 4095])
        self.c_stripidx = inp("c_stripidx", [NH, 128, 1], U32)
        self.c_ident = inp("c_ident", [128, 128])
        self.c_mask0 = inp("c_mask0", [128, 128])
        self.c_mask4 = inp("c_mask4", [128, 128])
        self.c_rot = inp("c_rot", [128, 128])
        self.c_ones2 = inp("c_ones2", [128, 2])
        self.c_pat2 = inp("c_pat2", [2, 128])
        self.c_ones1 = inp("c_ones1", [1, 128])
        self.c_onescol = inp("c_onescol", [128, 1])
        self.c_P8 = inp("c_P8", [8, 2])
        self.c_cs = inp("c_cs", [128, N]); self.c_ss = inp("c_ss", [128, N])
        self.c_ci = inp("c_ci", [128, N]); self.c_si = inp("c_si", [128, N])
        self.xout = nc.dram_tensor("xout", [2, 8, 128, N], F32, kind="ExternalOutput").ap()
        self.x_state = nc.dram_tensor("x_state", [2, 8, 128, N], F32, kind="Internal").ap()
        self.dv_dram = nc.dram_tensor("dv_dram", [NH * 4224, 1], BF16, kind="Internal").ap()
        self.dbg = {}
        if self.debug_stage:
            self.dbg["d1"] = nc.dram_tensor("dbg1", [8, 128, N], F32, kind="ExternalOutput").ap()

    def alloc(self, ctx):
        EN = ctx.enter_context
        nc = self.nc
        self.A = EN(nc.sbuf_tensor("arA", [128, 8 * ZW], F32))
        self.B = EN(nc.sbuf_tensor("arB", [128, 16896], BF16))
        self.C = EN(nc.sbuf_tensor("arC", [128, 16384], BF16))
        self.HLN = EN(nc.sbuf_tensor("arH", [128, 16384], BF16))
        self.Fw = EN(nc.sbuf_tensor("arF", [128, 6656], BF16))
        self.EXPB = EN(nc.sbuf_tensor("arE", [128, 2048], BF16))
        self.G = EN(nc.sbuf_tensor("arG", [128, 1856], BF16))
        self.Gf = EN(nc.sbuf_tensor("arGf", [128, 640], F32))
        self.R2 = EN(nc.sbuf_tensor("arR2", [2, 1536], F32))
        self.SC = EN(nc.sbuf_tensor("arSC", [1, 64], F32))
        self.EPS2 = EN(nc.sbuf_tensor("arEPS2", [2, 1], F32))
        self.R2b = EN(nc.sbuf_tensor("arR2b", [2, 512], BF16))
        self.R1 = EN(nc.sbuf_tensor("arR1", [1, 3072], F32))
        self.R1b = EN(nc.sbuf_tensor("arR1b", [1, 1024], BF16))
        self.ODD = EN(nc.sbuf_tensor("arODD", [64, 512], BF16))
        self.DVS = EN(nc.sbuf_tensor("arDVS", [16, 512], BF16))
        self.PSW = EN(nc.psum_tensor("PSW", [128, 1024], F32))
        self.PSX = EN(nc.psum_tensor("PSX", [128, 1024], F32))
        _ps2 = [EN(nc.psum_tensor(f"PS{i}", [128, 512], F32)) for i in range(2, 6)]
        class _PV:
            def __init__(s, ten, off):
                s.t = ten; s.off = off
            def __getitem__(s, key):
                p, c = key
                c0 = c.start if c.start is not None else 0
                c1 = c.stop if c.stop is not None else 512
                return s.t[p, s.off + c0:s.off + c1]
        self.PS = [_PV(self.PSW, 0), _PV(self.PSW, 512), _ps2[0], _ps2[1],
                   _ps2[2], _ps2[3], _PV(self.PSX, 0), _PV(self.PSX, 512)]

        self.g_, p = {}, 0
        for nm, w in [("ident", 128), ("mask0", 128), ("mask4", 128), ("rot", 128),
                      ("ones2", 2), ("pat2", 128), ("ones1", 128), ("onescol", 1),
                      ("P8", 2), ("gw2", 2), ("gbb", 32), ("ab", 1024)]:
            self.g_[nm] = (p, w); p += w
        assert p <= 1856, p
        self.f_, p = {}, 0
        for nm, w in [("bqcol", 8), ("bkcol", 8), ("bvcol", 8), ("bocol", 8),
                      ("b2col", 16), ("lngcol", 16), ("lnbcol", 16),
                      ("b1acol", 44), ("b1gcol", 44),
                      ("qscol", 2), ("kscol", 2), ("qsrot", 2), ("ksrot", 2),
                      ("gruccol", 16), ("signeg", 12), ("gates", 256),
                      ("bo2col", 8), ("ones1f", 128)]:
            self.f_[nm] = (p, w); p += w
        assert p <= 640, p

    def g(self, nm, p0=0, p1=128, j=0, w=None):
        c0, wd = self.g_[nm]
        if w is None:
            w = wd - j
        return self.G[p0:p1, c0 + j:c0 + j + w]

    def f(self, nm, p0=0, p1=128, j=0, w=None):
        c0, wd = self.f_[nm]
        if w is None:
            w = wd - j
        return self.Gf[p0:p1, c0 + j:c0 + j + w]

    def mk_sems(self):
        self.war = {}
        self.region_role = {}
        nc = self.nc
        self.sT = Sem(nc, "sT"); self.sV = Sem(nc, "sV"); self.sS = Sem(nc, "sS")
        self.dP = Sem(nc, "dP"); self.dV = Sem(nc, "dV")
        self.dX = Sem(nc, "dX"); self.dY = Sem(nc, "dY")

    def T(self, fn, waits=()):
        return self.sch.op("tensor", fn, waits, ((self.sT, 1),))
    def V(self, fn, waits=()):
        return self.sch.op("vector", fn, waits, ((self.sV, 1),))
    def S(self, fn, waits=()):
        return self.sch.op("scalar", fn, waits, ((self.sS, 1),))
    def Pd(self, fn, waits=()):
        return self.sch.op("gpsimd", fn, waits, ((self.dP, 16),))

    def wload(self, key, role, fn, extra=()):
        if self.region_role.get(key) != role:
            waits = [self.now(self.sT), self.now(self.sS), self.now(self.sV)]
            self.region_role[key] = role
        else:
            waits = [self.war[key]] if key in self.war else []
        return self.Pd(fn, list(waits) + [w for w in extra if w is not None])
    def Vd(self, fn, waits=()):
        return self.sch.op("gpsimd", fn, waits, ((self.dV, 16),))
    def Xd(self, fn, waits=()):
        return self.sch.op("scalar", fn, waits, ((self.dX, 16),))
    def Yd(self, fn, waits=()):
        return self.sch.op("sync", fn, waits, ((self.dY, 16),))
    def now(self, sem):
        return (sem, sem.n)
    def barrierTVS(self):
        return [self.now(self.sT), self.now(self.sV), self.now(self.sS)]

    def hf(self, c0, w):
        return self.HLN[:, :].bitcast(F32)[:, c0:c0 + w]

    def zsl(self, pt, t0, w):
        return self.A[:, pt * ZW + 1 + t0: pt * ZW + 1 + t0 + w]

    def ph0(self):
        g = self.g; f = self.f
        self.V(lambda e: e.memset(self.SC[0:1, 8:9], 1e-5))
        self.V(lambda e: e.memset(self.f("ones1f", 0, 1), 1.0))
        self.V(lambda e: e.memset(self.EPS2[0:2, 0:1], 1e-24))
        self.w_eps = self.now(self.sV)
        for nm, src, p1 in [("ident", self.c_ident, 128), ("mask0", self.c_mask0, 128),
                            ("mask4", self.c_mask4, 128), ("rot", self.c_rot, 128),
                            ("ones2", self.c_ones2, 128), ("pat2", self.c_pat2, 2),
                            ("ones1", self.c_ones1, 1), ("onescol", self.c_onescol, 128),
                            ("P8", self.c_P8, 8)]:
            self.Pd(lambda e, nm=nm, src=src, p1=p1: e.dma_start(out=self.g(nm, 0, p1), in_=src))
        w_const = self.now(self.dP)

        colv = lambda ap: ap.rearrange("(a b) -> a b", b=1)
        for pt in range(8):
            for nm, src in [("bqcol", self.battn["bq"]), ("bkcol", self.battn["bk"]),
                            ("bvcol", self.battn["bv"]), ("bocol", self.battn["bo"])]:
                self.Xd(lambda e, nm=nm, src=src, pt=pt: e.dma_start(
                    out=self.f(nm, 0, 128, pt, 1), in_=colv(src[pt * 128:(pt + 1) * 128])))
        for l in range(2):
            for pt in range(8):
                for nm, src in [("b2col", self.b2), ("lngcol", self.lng), ("lnbcol", self.lnb)]:
                    self.Xd(lambda e, nm=nm, src=src, l=l, pt=pt: e.dma_start(
                        out=self.f(nm, 0, 128, l * 8 + pt, 1),
                        in_=colv(src[l, pt * 128:(pt + 1) * 128])))
            for pt in range(22):
                for nm, src in [("b1acol", self.b1a), ("b1gcol", self.b1g)]:
                    self.Xd(lambda e, nm=nm, src=src, l=l, pt=pt: e.dma_start(
                        out=self.f(nm, 0, 128, l * 22 + pt, 1),
                        in_=colv(src[l, pt * 128:pt * 128 + 128])))
            for nm, src in [("qscol", self.qscol), ("kscol", self.kscol),
                            ("qsrot", self.qsrot), ("ksrot", self.ksrot)]:
                self.Xd(lambda e, nm=nm, src=src, l=l: e.dma_start(
                    out=self.f(nm, 0, 128, l, 1), in_=src[l]))
        w_cols = self.now(self.dX)

        for kt in range(3):
            h0 = kt * 128; h1 = min(NBUCK, h0 + 128)
            self.Pd(lambda e, kt=kt, h0=h0, h1=h1: e.dma_start(
                out=self.Fw[0:h1 - h0, kt * 16:kt * 16 + 16], in_=self.rel[h0:h1, :]))
        w_rel = self.now(self.dP)
        for ch in range(8):
            c0 = ch * 512; c1 = min(4095, c0 + 512); w = c1 - c0
            wS_ev = self.now(self.sS)
            for kt in range(3):
                h0 = kt * 128; h1 = min(NBUCK, h0 + 128)
                wld = self.Pd(lambda e, kt=kt, h0=h0, h1=h1, c0=c0, w=w, ch=ch: e.dma_start(
                    out=self.Fw[0:h1 - h0, 64 + (ch % 2) * 512:64 + (ch % 2) * 512 + w],
                    in_=self.c_S[h0:h1, c0:c0 + w]), waits=[self.now(self.sT)])
                self.T(lambda e, kt=kt, h0=h0, h1=h1, w=w, ch=ch: e.matmul(
                    self.PS[ch % 2][0:16, 0:w],
                    self.Fw[0:h1 - h0, kt * 16:kt * 16 + 16],
                    self.Fw[0:h1 - h0, 64 + (ch % 2) * 512:64 + (ch % 2) * 512 + w],
                    start=(kt == 0), stop=(kt == 2)),
                    waits=[wld, w_rel, wS_ev])
            wT = self.now(self.sT)
            self.S(lambda e, w=w, ch=ch: e.copy(self.DVS[:, 0:w], self.PS[ch % 2][0:16, 0:w]),
                   waits=[wT, self.now(self.dP)])
            wS2 = self.now(self.sS)
            self.Pd(lambda e, c0=c0, w=w: e.dma_start(
                out=AP(self.dv_dram.tensor, c0, [[4224, 16], [1, w]]),
                in_=self.DVS[:, 0:w]), waits=[wS2])
        self.w_dv = self.now(self.dP)

        self.Pd(lambda e: e.dma_start(out=self.Fw[0:8, 0:64], in_=self.gruW))
        self.Pd(lambda e: e.dma_start(out=self.Fw[0:8, 64:65], in_=self.grub))
        w_gru = self.now(self.dP)
        self.T(lambda e: e.matmul(self.PS[3][0:64, 0:2], self.Fw[0:8, 0:64],
                                  self.g("P8", 0, 8), start=True, stop=True),
               waits=[w_gru, w_const])
        self.T(lambda e: e.matmul(self.PS[3][0:1, 4:6], self.Fw[0:8, 64:65],
                                  self.g("P8", 0, 8), start=True, stop=True))
        wT = self.now(self.sT)
        self.S(lambda e: e.copy(self.g("gw2", 0, 64), self.PS[3][0:64, 0:2]), waits=[wT])
        self.S(lambda e: e.copy(self.R1b[0:1, 0:2], self.PS[3][0:1, 4:6]))
        wS = self.now(self.sS)
        self.Pd(lambda e: e.dma_start(out=self.g("gw2", 64, 128), in_=self.g("gw2", 0, 64)),
                waits=[wS])
        self.w_gw2 = self.now(self.dP)
        self.T(lambda e: e.matmul(
            self.PS[4][:, 0:32], self.g("ones1", 0, 1),
            self.R1b[0:1, 0:2][:, None, :].broadcast_to([1, 16, 2]),
            start=True, stop=True), waits=[wS])
        wT = self.now(self.sT)
        self.S(lambda e: e.copy(self.g("gbb", 0, 128), self.PS[4][:, 0:32]), waits=[wT])

        self.Xd(lambda e: e.dma_start(out=self.R1[0:1, 0:16], in_=self.gruc))
        w_gc = self.now(self.dX)
        self.V(lambda e: e.tensor_copy(self.R1b[0:1, 16:32], self.R1[0:1, 0:16]), waits=[w_gc])
        wV = self.now(self.sV)
        self.T(lambda e: e.matmul(self.PS[5][:, 0:16], self.g("ones1", 0, 1),
                                  self.R1b[0:1, 16:32], start=True, stop=True),
               waits=[wV, w_const])
        wT = self.now(self.sT)
        self.S(lambda e: e.copy(self.f("gruccol"), self.PS[5][:, 0:16]), waits=[wT])
        self.V(lambda e: e.tensor_reduce(self.SC[0:1, 10:11], self.R1[0:1, 0:16],
                                         axis=AX.X, op=ALU.max, apply_absolute_value=True),
               waits=[w_gc])
        relflat = self.rel.rearrange("a b -> (a b)").rearrange("(a b) -> a b", a=1)
        for ch in range(3):
            c0 = ch * 2048; c1 = min(5120, c0 + 2048)
            w_rl = self.Xd(lambda e, c0=c0, c1=c1: e.dma_start(
                out=self.R1[0:1, 0:c1 - c0], in_=relflat[:, c0:c1]),
                waits=[self.now(self.sV)])
            self.V(lambda e, ch=ch, w=c1 - c0: e.tensor_reduce(
                self.SC[0:1, 1 + ch:2 + ch], self.R1[0:1, 0:w],
                axis=AX.X, op=ALU.max, apply_absolute_value=True), waits=[w_rl])
        self.V(lambda e: e.tensor_reduce(self.SC[0:1, 4:5], self.SC[0:1, 1:4],
                                         axis=AX.X, op=ALU.max), waits=[self.now(self.sV)])
        self.V(lambda e: e.tensor_scalar(out=self.SC[0:1, 5:6], in0=self.SC[0:1, 10:11],
                                         scalar1=3.0, scalar2=None, op0=ALU.add),
               waits=[self.now(self.sV)])
        self.V(lambda e: e.tensor_tensor(out=self.SC[0:1, 6:7], in0=self.SC[0:1, 5:6],
                                         in1=self.SC[0:1, 4:5], op=ALU.mult),
               waits=[self.now(self.sV)])
        self.w_bterm = self.now(self.sV)

        for l in range(2):
            self.Xd(lambda e, l=l: e.dma_start(
                out=self.R1[0:1, 2560 + 128 * l:2624 + 128 * l], in_=self.qsrow[l]))
            self.Xd(lambda e, l=l: e.dma_start(
                out=self.R1[0:1, 2624 + 128 * l:2688 + 128 * l], in_=self.ksrow[l]))
        w_qs = self.now(self.dX)
        for l in range(2):
            self.V(lambda e, l=l: e.tensor_reduce(
                self.R1[0:1, 24:25], self.R1[0:1, 2560 + 128 * l:2624 + 128 * l],
                axis=AX.X, op=ALU.max, apply_absolute_value=True), waits=[w_qs])
            self.V(lambda e, l=l: e.tensor_reduce(
                self.R1[0:1, 25:26], self.R1[0:1, 2624 + 128 * l:2688 + 128 * l],
                axis=AX.X, op=ALU.max, apply_absolute_value=True))
            self.V(lambda e: e.tensor_tensor(out=self.R1[0:1, 26:27], in0=self.R1[0:1, 24:25],
                                             in1=self.R1[0:1, 25:26], op=ALU.mult),
                   waits=[self.now(self.sV)])
            self.V(lambda e, l=l: e.tensor_scalar(
                out=self.R1b[0:1, 40 + l:41 + l], in0=self.R1[0:1, 26:27],
                scalar1=-QK_SCALE, scalar2=None, op0=ALU.mult),
                   waits=[self.now(self.sV)])
            wV = self.now(self.sV)
            self.T(lambda e, l=l: e.matmul(self.PS[5][:, 20 + l:21 + l], self.g("ones1", 0, 1),
                                           self.R1b[0:1, 40 + l:41 + l], start=True, stop=True),
                   waits=[wV, self.now(self.sS)])
            wT = self.now(self.sT)
            self.S(lambda e, l=l: e.copy(self.f("signeg", 0, 128, l, 1),
                                         self.PS[5][:, 20 + l:21 + l]), waits=[wT])

        self.V(lambda e: e.tensor_copy(self.EXPB[:, 0:8], self.f("bvcol")), waits=[w_cols])
        w_bvb = self.now(self.sV)
        for oc in range(8):
            for kt in range(8):
                wld = self.Pd(lambda e, oc=oc, kt=kt: e.dma_start(
                    out=self.Fw[:, 1024 + (kt % 2) * 128:1152 + (kt % 2) * 128],
                    in_=self.WoT[kt * 128:(kt + 1) * 128, oc * 128:(oc + 1) * 128]),
                    waits=[self.now(self.sT)])
                self.T(lambda e, oc=oc, kt=kt: e.matmul(
                    self.PS[6][:, oc:oc + 1],
                    self.Fw[:, 1024 + (kt % 2) * 128:1152 + (kt % 2) * 128],
                    self.EXPB[:, kt:kt + 1],
                    start=(kt == 0), stop=(kt == 7)), waits=[wld, w_bvb])
        wT = self.now(self.sT)
        self.V(lambda e: e.tensor_tensor(out=self.f("bo2col"), in0=self.PS[6][:, 0:8],
                                         in1=self.f("bocol"), op=ALU.add), waits=[wT])

    def ln_boundary(self, x_src, write_out):
        g = self.g
        XB = lambda pt: self.HLN[:, pt * 512:(pt + 1) * 512]
        SQ = lambda i: self.HLN[:, 4096 + (i % 2) * 512:4096 + (i % 2) * 512 + 512]
        TT = self.HLN[:, 5120:5632]
        for tc in range(NTC):
            wpre = self.barrierTVS() + [self.now(self.dY)]
            for pt in range(NPT):
                w = self.Vd(lambda e, pt=pt, tc=tc: e.dma_start(
                    out=XB(pt), in_=x_src[pt, :, tc * TC:(tc + 1) * TC]), waits=wpre)
            wall = w
            wmm7 = {}
            for pt in range(NPT):
                self.T(lambda e, pt=pt: e.matmul(
                    self.PS[6][0:1, 0:512], g("onescol"), XB(pt),
                    start=(pt == 0), stop=(pt == NPT - 1)),
                    waits=[wall] + ([self.now(self.sV)] if pt == 0 else []))
                self.S(lambda e, pt=pt: e.activation(out=SQ(pt), in_=XB(pt), func=AF.Square),
                       waits=[wall] + ([wmm7[pt - 2]] if pt >= 2 else []))
                wS = self.now(self.sS)
                self.T(lambda e, pt=pt: e.matmul(
                    self.PS[7][0:1, 0:512], g("onescol"), SQ(pt),
                    start=(pt == 0), stop=(pt == NPT - 1)), waits=[wS])
                wmm7[pt] = self.now(self.sT)
            wT = self.now(self.sT)
            wab = self.stats_to_ab(wT)
            TTf = self.hf(2560, 512)
            XF = lambda i: self.hf(3072 + (i % 2) * 512, 512)
            for pt in range(NPT):
                wxf = self.Xd(lambda e, pt=pt, tc=tc: e.dma_start(
                    out=XF(pt), in_=x_src[pt, :, tc * TC:(tc + 1) * TC]),
                    waits=[self.now(self.sV), self.now(self.dY)])
                self.V(lambda e, pt=pt, TTf=TTf: e.tensor_tensor(
                    out=TTf, in0=XF(pt), in1=self.PS[4][:, 0:512], op=ALU.mult),
                    waits=[wxf] + ([wab] if pt == 0 else []))
                write_out(pt, tc, TTf, self.PS[5][:, 0:512])

    def stats_to_ab(self, wT, gain=None, bias=None):
        g = self.g
        rr = self.R1
        self.V(lambda e: e.tensor_scalar(out=rr[0:1, 0:512], in0=self.PS[6][0:1, 0:512],
                                         scalar1=1.0 / 1024, scalar2=None, op0=ALU.mult),
               waits=[wT])
        self.V(lambda e: e.tensor_scalar(out=rr[0:1, 512:1024], in0=self.PS[7][0:1, 0:512],
                                         scalar1=1.0 / 1024, scalar2=None, op0=ALU.mult))
        self.V(lambda e: e.tensor_tensor(out=rr[0:1, 1024:1536], in0=rr[0:1, 0:512],
                                         in1=rr[0:1, 0:512], op=ALU.mult))
        self.V(lambda e: e.tensor_tensor(out=rr[0:1, 1536:2048], in0=rr[0:1, 512:1024],
                                         in1=rr[0:1, 1024:1536], op=ALU.subtract))
        wV = self.now(self.sV)
        self.S(lambda e: e.activation(out=rr[0:1, 2048:2560], in_=rr[0:1, 1536:2048],
                                      func=AF.Ln, bias=self.SC[0:1, 8:9], scale=1.0),
               waits=[wV, self.w_eps])
        self.S(lambda e: e.activation(out=rr[0:1, 2560:3072], in_=rr[0:1, 2048:2560],
                                      func=AF.Exp, scale=-0.5))
        wS = self.now(self.sS)
        self.V(lambda e: e.tensor_tensor(out=rr[0:1, 1024:1536], in0=rr[0:1, 0:512],
                                         in1=rr[0:1, 2560:3072], op=ALU.mult), waits=[wS])
        self.V(lambda e: e.tensor_scalar(out=rr[0:1, 512:1024], in0=rr[0:1, 1024:1536],
                                         scalar1=-1.0, scalar2=None, op0=ALU.mult))
        wV = self.now(self.sV)
        self.T(lambda e: e.matmul(self.PS[4][:, 0:512], self.f("ones1f", 0, 1),
                                  rr[0:1, 2560:3072], start=True, stop=True), waits=[wV])
        self.T(lambda e: e.matmul(self.PS[5][:, 0:512], self.f("ones1f", 0, 1),
                                  rr[0:1, 512:1024], start=True, stop=True))
        return self.now(self.sT)

    def body(self):
        self.ph0()
        if self.debug_stage == 54:
            wpre = [self.now(self.sS), self.now(self.sV), self.now(self.sT), self.now(self.dP)]
            self.Yd(lambda e: e.dma_start(out=self.dbg["d1"][0][0:1, 0:1536], in_=self.R1[0:1, 0:1536]), waits=wpre)
            self.Yd(lambda e: e.dma_start(out=self.dbg["d1"][0][1:2, 0:1536], in_=self.R1[0:1, 1536:3072]), waits=wpre)
            self.Yd(lambda e: e.dma_start(out=self.dbg["d1"][1][0:1, 0:64], in_=self.SC[:, :]), waits=wpre)
            self.Yd(lambda e: e.dma_start(out=self.dbg["d1"][2][:, 0:12], in_=self.f("signeg")), waits=wpre)
            self.Pd(lambda e: e.dma_start(out=self.dbg["d1"][3][0:1, 0:1024], in_=self.R1b[:, :]), waits=wpre)
            self.final_waits.append(self.now(self.dY))
            self.final_waits.append(self.now(self.dP))
            return
        nb = 1 if self.debug_stage else 2
        for b in range(nb):
            self.seq(b)

    def seq(self, b):
        self.zero_guard()
        def wz(pt, tc, t_ap, b_ap):
            self.V(lambda e, pt=pt, tc=tc: e.tensor_tensor(
                out=self.zsl(pt, tc * TC, TC), in0=t_ap, in1=b_ap, op=ALU.add))
        self.ln_boundary(self.xin[b], wz)
        if self.debug_stage == 1:
            self.dump_z(); return
        if self.debug_stage in (21, 22, 23, 24):
            l = 0
            if self.debug_stage == 22:
                self.ffn_mlp(l)
                self.dump_z(); return
            if self.debug_stage == 24:
                self.mlp_hln_only = True
                self.ffn_mlp(l)
                wS = self.now(self.sS)
                for pt in range(NPT):
                    self.Pd(lambda e, pt=pt: e.dma_start(
                        out=self.dbg["d1"][pt], in_=self.HLN[:, pt * 2048:(pt + 1) * 2048]),
                        waits=[wS])
                self.final_waits.append(self.now(self.dP))
                return
            self.ffn_qkv(l)
            self.ffn_rotary(l)
            if self.debug_stage == 23:
                self.dump_qk(); return
            self.ffn_local_attn(l)
            self.dump_z(); return
        nlayers = 2
        for l in range(nlayers):
            self.ffn_layer(b, l)
            if self.debug_stage == 2 and l == 0:
                self.dump_z(); return
        if self.debug_stage == 3:
            self.dump_z(); return
        self.residual_x(self.xin[b], self.x_state[b])
        if self.debug_stage == 4:
            self.dump_dram(self.x_state[b]); return
        if self.debug_stage in (51, 52, 53, 55, 56, 57, 58):
            self.attn_dbg = self.debug_stage
        self.attn_sublayer(b)
        if self.debug_stage == 5:
            self.dump_dram(self.x_state[b]); return
        if self.debug_stage in (51, 52, 53, 55, 56, 57, 58):
            return
        self.zero_guard()
        def wz2(pt, tc, t_ap, b_ap):
            self.V(lambda e, pt=pt, tc=tc: e.tensor_tensor(
                out=self.zsl(pt, tc * TC, TC), in0=t_ap, in1=b_ap, op=ALU.add))
        self.ln_boundary(self.x_state[b], wz2)
        for l in range(2):
            self.ffn_layer(b, l)
        self.residual_x(self.x_state[b], self.xout[b])

    def zero_guard(self):
        wpre = [self.now(self.sT), self.now(self.sS), self.now(self.dV)]
        for pt in range(NPT):
            self.V(lambda e, pt=pt: e.memset(self.A[:, pt * ZW:pt * ZW + 1], 0.0), waits=wpre)

    def dump_dram(self, src):
        w = self.final_waits[-1] if self.final_waits else self.now(self.dY)
        for pt in range(NPT):
            self.Yd(lambda e, pt=pt: e.dma_start(out=self.dbg["d1"][pt], in_=src[pt]), waits=[w])
        self.final_waits.append(self.now(self.dY))

    def dump_qk(self):
        wV = self.now(self.sV)
        for pt in range(NPT):
            self.Pd(lambda e, pt=pt: e.dma_start(
                out=self.dbg["d1"][pt], in_=self.C[:, pt * 2048:(pt + 1) * 2048]), waits=[wV])
        self.final_waits.append(self.now(self.dP))

    def dump_z(self):
        wV = self.now(self.sV)
        for pt in range(NPT):
            self.Yd(lambda e, pt=pt: e.dma_start(out=self.dbg["d1"][pt], in_=self.zsl(pt, 0, N)),
                    waits=[wV])
        self.final_waits.append(self.now(self.dY))

    def residual_x(self, x_src, x_dst):
        XC = lambda i: self.hf(4096 + (i % 2) * 512, 512)
        RC = lambda i: self.hf(5120 + (i % 2) * 512, 512)
        wpre = self.barrierTVS()
        for pt in range(NPT):
            for tc in range(NTC):
                i = pt * NTC + tc
                w = self.Xd(lambda e, pt=pt, tc=tc, i=i: e.dma_start(
                    out=XC(i), in_=x_src[pt, :, tc * TC:(tc + 1) * TC]),
                    waits=wpre + [self.now(self.sV), self.now(self.dY)])
                self.V(lambda e, pt=pt, tc=tc, i=i: e.scalar_tensor_tensor(
                    out=RC(i), in0=self.zsl(pt, tc * TC, TC), scalar=0.5,
                    in1=XC(i), op0=ALU.mult, op1=ALU.add), waits=[w])
                wV = self.now(self.sV)
                self.Yd(lambda e, pt=pt, tc=tc, i=i: e.dma_start(
                    out=x_dst[pt, :, tc * TC:(tc + 1) * TC], in_=RC(i)), waits=[wV])
        self.final_waits.append(self.now(self.dY))

    def ffn_layer(self, b, l):
        self.ffn_qkv(l)
        self.ffn_rotary(l)
        self.ffn_local_attn(l)
        self.ffn_mlp(l)

    def conv_chunk(self, tc, waits=()):
        return self.Vd(lambda e, tc=tc: e.dma_start(
            out=self.Fw[:, 0:4104],
            in_=self.A[:, :].rearrange("p (q c) -> p q c", q=8)[:, :, tc * TC:tc * TC + 513]),
            waits=list(waits))

    def ffn_qkv(self, l):
        g = self.g
        WSL = lambda s: self.Fw[:, 4104 + s * 1024:4104 + s * 1024 + 1024]
        VW = self.HLN[:, 12288:12288 + 4096]
        self._vwctr = getattr(self, '_vwctr', 0) + 1
        wvw = self.wload("VW", f"ffnv{self._vwctr}", lambda e, l=l: e.dma_start(
            out=VW, in_=self.wqkv_vw[l]))
        conv = lambda pt: self.Fw[:, pt * 513:pt * 513 + 513]
        for tc in range(NTC):
            wcv = self.conv_chunk(tc, waits=[self.now(self.sT), self.now(self.sV)])
            for oc in range(8):
                wld = self.wload(("FW", oc % 2), "qkv", lambda e, oc=oc, l=l: e.dma_start(
                    out=WSL(oc % 2), in_=self.wqkv_slab[l, oc]))
                for kt in range(8):
                    sh = 1 if kt >= 4 else 0
                    self.T(lambda e, oc=oc, kt=kt, sh=sh: e.matmul(
                        self.PS[oc % 2][:, 0:512],
                        WSL(oc % 2)[:, kt * 128:kt * 128 + 128],
                        conv(kt)[:, 1 - sh:513 - sh],
                        start=(kt == 0), stop=(kt == 7)),
                        waits=[wld, wcv] + ([self.now(self.sS)] if kt == 0 else []))
                wT = self.now(self.sT)
                self.war[("FW", oc % 2)] = wT
                self.S(lambda e, oc=oc, tc=tc: e.copy(
                    self.C[:, oc * 2048 + tc * 512:oc * 2048 + tc * 512 + 512],
                    self.PS[oc % 2][:, 0:512]), waits=[wT])
            for ktok in range(4):
                kt_g = tc * 4 + ktok
                for kt in range(8):
                    sh = 1 if kt >= 4 else 0
                    self.T(lambda e, kt=kt, ktok=ktok, sh=sh: e.matmul(
                        self.PS[2 + (ktok % 2)][:, 0:512],
                        conv(kt)[:, 1 - sh + ktok * 128:1 - sh + ktok * 128 + 128],
                        VW[:, kt * 512:kt * 512 + 512],
                        start=(kt == 0), stop=(kt == 7)),
                        waits=[wcv, wvw] + ([self.now(self.sS)] if kt == 0 else []))
                wT = self.now(self.sT)
                self.war["VW"] = wT
                self.S(lambda e, kt_g=kt_g, ktok=ktok: e.copy(
                    self.B[:, kt_g * 528:kt_g * 528 + 528]
                        .rearrange("p (h c) -> p h c", h=8)[:, :, 0:64],
                    self.PS[2 + (ktok % 2)][:, 0:512]), waits=[wT])
        wS = self.now(self.sS)
        self.V(lambda e: e.memset(
            self.B[:, 0:16 * 528].rearrange("p (x c) -> p x c", c=66)[:, :, 64:65], 1.0),
            waits=[wS])

    def ffn_rotary(self, l):
        g = self.g
        tables = (self.c_cs, self.c_ss, self.c_ci, self.c_si)
        order = [("qscol", 0, 0), ("qsrot", 1, 1), ("kscol", 2, 2), ("ksrot", 3, 3)]
        for j, (cn, ti, _) in enumerate(order):
            wtb = self.Pd(lambda e, ti=ti: e.dma_start(
                out=self.HLN[:, 8192 + (ti % 2) * 2048:8192 + (ti % 2) * 2048 + 2048],
                in_=tables[ti]),
                waits=[self.now(self.sT), self.now(self.sS), self.now(self.sV)])
            self.S(lambda e, j=j, cn=cn, ti=ti, l=l: e.activation(
                out=self.HLN[:, j * 2048:(j + 1) * 2048],
                in_=self.HLN[:, 8192 + (ti % 2) * 2048:8192 + (ti % 2) * 2048 + 2048],
                func=AF.Identity, scale=self.f(cn, 0, 128, l, 1)),
                waits=[wtb])
        wprep = self.now(self.sS)
        scr1 = self.HLN[:, 8192:10240]
        scr2 = self.HLN[:, 10240:10752]
        scr3 = self.HLN[:, 10752:11264]
        for qk in range(2):
            coff = 0 if qk == 0 else 4096
            for pti in range(4):
                src = self.C[:, (qk * 4 + pti) * 2048:(qk * 4 + pti) * 2048 + 2048]
                self.S(lambda e, src=src: e.activation(out=scr1, in_=src, func=AF.Square),
                       waits=[self.now(self.sV), self.now(self.sT)])
                wS = self.now(self.sS)
                for tc in range(NTC):
                    self.T(lambda e, tc=tc: e.matmul(
                        self.PS[6 + (tc % 2)][0:2, 0:512], g("ones2"),
                        scr1[:, tc * 512:tc * 512 + 512], start=True, stop=True),
                        waits=[wS, self.now(self.sV), self.now(self.sS)])
                    wT = self.now(self.sT)
                    self.S(lambda e, tc=tc: e.activation(
                        out=self.R2[0:2, 0:512], in_=self.PS[6 + (tc % 2)][0:2, 0:512],
                        func=AF.Ln, bias=self.EPS2[0:2, 0:1], scale=1.0),
                        waits=[wT, self.w_eps])
                    self.S(lambda e: e.activation(out=self.R2[0:2, 512:1024],
                                                  in_=self.R2[0:2, 0:512], func=AF.Exp, scale=-0.5))
                    wS2 = self.now(self.sS)
                    self.V(lambda e: e.tensor_copy(self.R2b[0:2, 0:512], self.R2[0:2, 512:1024]),
                           waits=[wS2])
                    wV = self.now(self.sV)
                    self.T(lambda e: e.matmul(self.PS[4][:, 0:512], g("pat2", 0, 2),
                                              self.R2b[0:2, 0:512], start=True, stop=True),
                           waits=[wV])
                    self.T(lambda e, src=src, tc=tc: e.matmul(
                        self.PS[5][:, 0:512], g("rot"), src[:, tc * 512:tc * 512 + 512],
                        start=True, stop=True), waits=[self.now(self.sV)])
                    wT2 = self.now(self.sT)
                    self.V(lambda e, src=src, tc=tc, coff=coff: e.tensor_tensor(
                        out=scr2, in0=src[:, tc * 512:tc * 512 + 512],
                        in1=self.HLN[:, coff + tc * 512:coff + tc * 512 + 512], op=ALU.mult),
                        waits=[wprep, wT2])
                    self.V(lambda e, tc=tc, coff=coff: e.tensor_tensor(
                        out=scr3, in0=self.PS[5][:, 0:512],
                        in1=self.HLN[:, coff + 2048 + tc * 512:coff + 2048 + tc * 512 + 512],
                        op=ALU.mult))
                    self.V(lambda e: e.tensor_tensor(out=scr2, in0=scr2, in1=scr3, op=ALU.add))
                    self.V(lambda e, src=src, tc=tc: e.tensor_tensor(
                        out=src[:, tc * 512:tc * 512 + 512], in0=scr2,
                        in1=self.PS[4][:, 0:512], op=ALU.mult))

    def ffn_local_attn(self, l):
        g = self.g
        OPT = 12416
        RB = self.HLN[0:64, 8192:8704]
        for qc in range(NTC):
            for h in range(LH):
                base = 64 * (h % 2)
                pti = h // 2
                tp = (64, 0) if base else None
                qsl = self.C[base:base + 64, pti * 2048 + qc * 512:pti * 2048 + qc * 512 + 512]
                kts = list(range(max(0, 4 * qc - 4), 4 * qc + 4))
                kts.sort(key=lambda kt: (0 if kt == 4 * qc else 1, kt))
                wexp_prev = {}
                wpv_prev = {}
                for i, kt in enumerate(kts):
                    ktr = kt - 4 * qc
                    j0 = max(0, ktr); j1 = min(3, ktr + 4)
                    c0 = 128 * j0; c1 = 128 * (j1 + 1)
                    pb = i % 2
                    ksl = self.C[base:base + 64,
                                 (4 + pti) * 2048 + kt * 128:(4 + pti) * 2048 + kt * 128 + 128]
                    mask_js = []
                    if ktr <= -1 and ktr + 4 <= 3:
                        mask_js.append(("mask4", ktr + 4))
                    if ktr >= 0:
                        mask_js.append(("mask0", ktr))
                    wW = [wexp_prev[pb]] if pb in wexp_prev else [self.now(self.sS)]
                    self.T(lambda e, ksl=ksl, qsl=qsl, pb=pb, c0=c0, c1=c1, tp=tp,
                           st=(len(mask_js) == 0): e.matmul(
                        self.PS[pb][:, c0:c1], ksl, qsl[:, c0:c1],
                        start=True, stop=st, tile_position=tp), waits=wW)
                    for mi, (mn, jm) in enumerate(mask_js):
                        self.T(lambda e, pb=pb, mn=mn, jm=jm, st=(mi == len(mask_js) - 1): e.matmul(
                            self.PS[pb][:, jm * 128:jm * 128 + 128], g(mn), g("ident"),
                            start=False, stop=st))
                    wT = self.now(self.sT)
                    self.S(lambda e, pb=pb, c0=c0, c1=c1, l=l: e.activation(
                        out=self.EXPB[:, pb * 1024 + c0:pb * 1024 + c1],
                        in_=self.PS[pb][:, c0:c1], func=AF.Exp,
                        bias=self.f("signeg", 0, 128, l, 1), scale=QK_SCALE), waits=[wT])
                    wexp = self.now(self.sS)
                    wexp_prev[pb] = wexp
                    self.T(lambda e, kt=kt, h=h, pb=pb, c0=c0, c1=c1,
                           fp=(i == 0), lp=(i == len(kts) - 1): e.matmul(
                        self.PS[2][0:65, c0:c1],
                        self.B[:, kt * 528 + h * 66:kt * 528 + h * 66 + 65],
                        self.EXPB[:, pb * 1024 + c0:pb * 1024 + c1],
                        start=fp, stop=lp),
                        waits=[wexp] + ([self.now(self.sV)] if i == 0 else []))
                wTpv = self.now(self.sT)
                self.V(lambda e: e.tensor_scalar(
                    out=self.R2[0:1, 1024:1536], in0=self.PS[2][64:65, 0:512],
                    scalar1=1e-30, scalar2=None, op0=ALU.add), waits=[wTpv])
                wVd_ = self.now(self.sV)
                self.S(lambda e: e.activation(out=self.R2[0:1, 0:512], in_=self.R2[0:1, 1024:1536],
                                              func=AF.Ln), waits=[wVd_])
                self.S(lambda e: e.activation(out=self.R2b[0:1, 0:512], in_=self.R2[0:1, 0:512],
                                              func=AF.Exp, scale=-1.0))
                wV = self.now(self.sS)
                self.T(lambda e: e.matmul(self.PS[4][0:64, 0:512], g("pat2", 0, 2)[0:1, 0:64],
                                          self.R2b[0:1, 0:512], start=True, stop=True),
                       waits=[wV, self.now(self.sS)])
                wT2 = self.now(self.sT)
                self.S(lambda e: e.copy(RB, self.PS[4][0:64, 0:512]), waits=[wT2])
                wS3 = self.now(self.sS)
                if h % 2 == 0:
                    self.V(lambda e, pti=pti: e.tensor_tensor(
                        out=self.B[0:64, OPT + pti * 512:OPT + pti * 512 + 512],
                        in0=self.PS[2][0:64, 0:512], in1=RB, op=ALU.mult),
                        waits=[wS3, self.now(self.sT)])
                else:
                    self.V(lambda e: e.tensor_tensor(
                        out=self.ODD[0:64, 0:512], in0=self.PS[2][0:64, 0:512],
                        in1=RB, op=ALU.mult), waits=[wS3, self.now(self.dP), self.now(self.sT)])
                    wV2 = self.now(self.sV)
                    self.Pd(lambda e, pti=pti: e.dma_start(
                        out=self.B[64:128, OPT + pti * 512:OPT + pti * 512 + 512],
                        in_=self.ODD[0:64, 0:512]), waits=[wV2])
            wodd = self.now(self.dP)
            wvo = self.now(self.sV)
            for oc in range(8):
                wld = self.wload(("FW", oc % 2), "wo", lambda e, oc=oc, l=l: e.dma_start(
                    out=self.Fw[:, 4104 + (oc % 2) * 1024:4104 + (oc % 2) * 1024 + 512],
                    in_=self.wo_slab[l, oc]))
                for kt in range(4):
                    self.T(lambda e, oc=oc, kt=kt: e.matmul(
                        self.PS[5 + (oc % 2)][:, 0:512],
                        self.Fw[:, 4104 + (oc % 2) * 1024 + kt * 128:4104 + (oc % 2) * 1024 + kt * 128 + 128],
                        self.B[:, OPT + kt * 512:OPT + kt * 512 + 512],
                        start=(kt == 0), stop=(kt == 3)),
                        waits=[wld, wodd, wvo] + ([self.now(self.sV)] if kt == 0 else []))
                wT = self.now(self.sT)
                self.war[("FW", oc % 2)] = wT
                self.V(lambda e, oc=oc, qc=qc: e.tensor_tensor(
                    out=self.zsl(oc, qc * 512, 512), in0=self.zsl(oc, qc * 512, 512),
                    in1=self.PS[5 + (oc % 2)][:, 0:512], op=ALU.add), waits=[wT])

    def ffn_mlp(self, l):
        g = self.g
        conv = lambda pt: self.Fw[:, pt * 513:pt * 513 + 513]
        SQ = lambda i: self.B[:, 12416 + (i % 2) * 512:12416 + (i % 2) * 512 + 512]
        for tc in range(NTC):
            wcv = self.conv_chunk(tc, waits=[self.now(self.sT), self.now(self.sS), self.now(self.sV)])
            wmm7 = {}
            for pt in range(NPT):
                sh = 1 if pt >= 4 else 0
                self.T(lambda e, pt=pt, sh=sh: e.matmul(
                    self.PS[6][0:1, 0:512], g("onescol"), conv(pt)[:, 1 - sh:513 - sh],
                    start=(pt == 0), stop=(pt == NPT - 1)), waits=[wcv])
                self.S(lambda e, pt=pt, sh=sh: e.activation(
                    out=SQ(pt), in_=conv(pt)[:, 1 - sh:513 - sh], func=AF.Square),
                    waits=[wcv] + ([wmm7[pt - 2]] if pt >= 2 else []))
                wS = self.now(self.sS)
                self.T(lambda e, pt=pt: e.matmul(
                    self.PS[7][0:1, 0:512], g("onescol"), SQ(pt),
                    start=(pt == 0), stop=(pt == NPT - 1)), waits=[wS])
                wmm7[pt] = self.now(self.sT)
            wT = self.now(self.sT)
            wab = self.stats_to_ab(wT)
            wS_prev = {}
            for pt in range(NPT):
                sh = 1 if pt >= 4 else 0
                TTf = self.B[:, 13440 + (pt % 2) * 1024:14464 + (pt % 2) * 1024].bitcast(F32)
                zin = self.A[:, pt * ZW + tc * TC + 1 - sh:pt * ZW + tc * TC + 1 - sh + 512]
                self.V(lambda e, pt=pt, zin=zin, TTf=TTf: e.tensor_tensor(
                    out=TTf, in0=zin, in1=self.PS[4][:, 0:512], op=ALU.mult),
                    waits=([wab] if pt == 0 else []) + ([wS_prev[pt - 2]] if pt >= 2 else []))
                self.V(lambda e, TTf=TTf: e.tensor_tensor(
                    out=TTf, in0=TTf, in1=self.PS[5][:, 0:512], op=ALU.add))
                wV = self.now(self.sV)
                self.S(lambda e, pt=pt, tc=tc, l=l, TTf=TTf: e.activation(
                    out=self.HLN[:, pt * 2048 + tc * 512:pt * 2048 + tc * 512 + 512],
                    in_=TTf, func=AF.Identity,
                    scale=self.f("lngcol", 0, 128, l * 8 + pt, 1),
                    bias=self.f("lnbcol", 0, 128, l * 8 + pt, 1)), waits=[wV])
                wS_prev[pt] = self.now(self.sS)
        if getattr(self, 'mlp_hln_only', False):
            return
        W1S = lambda s: self.Fw[:, 4104 + s * 1024:4104 + s * 1024 + 1024]
        nslab = 22
        for tc in range(NTC):
            whl = self.now(self.sS)
            for sl in range(nslab):
                wlda = self.wload(("FW", 0), "w1a", lambda e, sl=sl, l=l: e.dma_start(
                    out=W1S(0), in_=self.w1a_slab[l, sl]))
                wldg = self.wload(("FW", 1), "w1g", lambda e, sl=sl, l=l: e.dma_start(
                    out=W1S(1), in_=self.w1g_slab[l, sl]))
                for kt in range(8):
                    self.T(lambda e, sl=sl, kt=kt, tc=tc: e.matmul(
                        self.PS[0][:, 0:512], W1S(0)[:, kt * 128:kt * 128 + 128],
                        self.HLN[:, kt * 2048 + tc * 512:kt * 2048 + tc * 512 + 512],
                        start=(kt == 0), stop=(kt == 7)),
                        waits=[wlda, whl] + ([self.now(self.sS), self.now(self.sV)] if kt == 0 else []))
                self.war[("FW", 0)] = self.now(self.sT)
                for kt in range(8):
                    self.T(lambda e, sl=sl, kt=kt, tc=tc: e.matmul(
                        self.PS[1][:, 0:512], W1S(1)[:, kt * 128:kt * 128 + 128],
                        self.HLN[:, kt * 2048 + tc * 512:kt * 2048 + tc * 512 + 512],
                        start=(kt == 0), stop=(kt == 7)),
                        waits=[wldg] + ([self.now(self.sS), self.now(self.sV)] if kt == 0 else []))
                wT = self.now(self.sT)
                self.war[("FW", 1)] = wT
                self.S(lambda e, sl=sl, l=l: e.activation(
                    out=self.EXPB[:, 0:512], in_=self.PS[0][:, 0:512], func=AF.Identity,
                    bias=self.f("b1acol", 0, 128, l * 22 + sl, 1)), waits=[wT])
                self.S(lambda e, sl=sl, l=l: e.activation(
                    out=self.EXPB[:, 512:1024], in_=self.PS[1][:, 0:512], func=AF.Gelu,
                    bias=self.f("b1gcol", 0, 128, l * 22 + sl, 1)))
                wS = self.now(self.sS)
                self.V(lambda e, sl=sl: e.tensor_tensor(
                    out=self.B[:, sl * 512:sl * 512 + 512],
                    in0=self.EXPB[:, 0:512], in1=self.EXPB[:, 512:1024],
                    op=ALU.mult), waits=[wS])
            wh1 = self.now(self.sV)
            for oc in range(8):
                wldk = self.wload(("FWC", oc % 2), "w2", lambda e, oc=oc, l=l: e.dma_start(
                    out=self.Fw[:, (oc % 2) * 2816:(oc % 2) * 2816 + 2816],
                    in_=self.w2t[l, oc]),
                    extra=[self.war.get(("FW", 0)), self.war.get(("FW", 1))])
                for kt in range(22):
                    self.T(lambda e, oc=oc, kt=kt: e.matmul(
                        self.PS[2 + (oc % 2)][:, 0:512],
                        self.Fw[:, (oc % 2) * 2816 + kt * 128:(oc % 2) * 2816 + kt * 128 + 128],
                        self.B[:, kt * 512:kt * 512 + 512],
                        start=(kt == 0), stop=(kt == 21)),
                        waits=[wldk, wh1] + ([self.now(self.sV)] if kt == 0 else []))
                wT = self.now(self.sT)
                self.war[("FWC", oc % 2)] = wT
                self.war[("FW", 0)] = wT
                self.war[("FW", 1)] = wT
                self.V(lambda e, oc=oc, tc=tc, l=l: e.scalar_tensor_tensor(
                    out=self.zsl(oc, tc * 512, 512), in0=self.PS[2 + (oc % 2)][:, 0:512],
                    scalar=self.f("b2col", 0, 128, l * 8 + oc, 1),
                    in1=self.zsl(oc, tc * 512, 512), op0=ALU.add, op1=ALU.add), waits=[wT])

    def attn_sublayer(self, b):
        g = self.g; f = self.f
        Abf = self.A[:, :].bitcast(BF16)
        QT = lambda pt: Abf[:, pt * 2048:(pt + 1) * 2048]
        KT = lambda pt: Abf[:, 16384 + pt * 2048:16384 + (pt + 1) * 2048]
        YC = lambda pt: self.Fw[:, pt * 512:pt * 512 + 512]
        WSL = lambda s: self.Fw[:, 4104 + s * 1024:4104 + s * 1024 + 1024]
        XB = lambda pt: self.HLN[:, pt * 512:(pt + 1) * 512]
        SQ = lambda i: self.HLN[:, 4096 + (i % 2) * 512:4096 + (i % 2) * 512 + 512]
        TT = self.HLN[:, 5120:5632]
        WVT = self.HLN[:, 8192:16384]
        self._vwctr = getattr(self, '_vwctr', 0) + 1
        wvw = self.wload("WVT", f"attnv{self._vwctr}", lambda e: e.dma_start(
            out=WVT, in_=self.WvTr))
        for tc in range(NTC):
            wpre = self.barrierTVS() + [self.now(self.dY)]
            for pt in range(NPT):
                w = self.Vd(lambda e, pt=pt, tc=tc, b=b: e.dma_start(
                    out=XB(pt), in_=self.x_state[b][pt, :, tc * TC:(tc + 1) * TC]), waits=wpre)
            wall = w
            wmm7 = {}
            for pt in range(NPT):
                self.T(lambda e, pt=pt: e.matmul(
                    self.PS[6][0:1, 0:512], g("onescol"), XB(pt),
                    start=(pt == 0), stop=(pt == NPT - 1)),
                    waits=[wall] + ([self.now(self.sV)] if pt == 0 else []))
                self.S(lambda e, pt=pt: e.activation(out=SQ(pt), in_=XB(pt), func=AF.Square),
                       waits=[wall] + ([wmm7[pt - 2]] if pt >= 2 else []))
                wS = self.now(self.sS)
                self.T(lambda e, pt=pt: e.matmul(
                    self.PS[7][0:1, 0:512], g("onescol"), SQ(pt),
                    start=(pt == 0), stop=(pt == NPT - 1)), waits=[wS])
                wmm7[pt] = self.now(self.sT)
            wT = self.now(self.sT)
            wab = self.stats_to_ab(wT)
            TTf = self.hf(2560, 512)
            XF = lambda i: self.hf(3072 + (i % 2) * 512, 512)
            for pt in range(NPT):
                wxf = self.Xd(lambda e, pt=pt, tc=tc, b=b: e.dma_start(
                    out=XF(pt), in_=self.x_state[b][pt, :, tc * TC:(tc + 1) * TC]),
                    waits=[self.now(self.sV), self.now(self.dY)])
                self.V(lambda e, pt=pt, TTf=TTf: e.tensor_tensor(
                    out=TTf, in0=XF(pt), in1=self.PS[4][:, 0:512], op=ALU.mult),
                    waits=[wxf] + ([wab, self.now(self.sT)] if pt == 0 else []))
                self.V(lambda e, pt=pt, TTf=TTf: e.tensor_tensor(
                    out=YC(pt), in0=TTf, in1=self.PS[5][:, 0:512], op=ALU.add))
            wy = self.now(self.sV)
            for which, Wnm, bcol, dst in ((0, "Wq", "bqcol", QT),
                                          (1, "Wk", "bkcol", KT)):
                for oc in range(8):
                    wld = self.wload(("FW", oc % 2), "attnqk", lambda e, oc=oc, Wnm=Wnm: e.dma_start(
                        out=WSL(oc % 2), in_=self.Wslab[Wnm][oc]))
                    for kt in range(8):
                        self.T(lambda e, oc=oc, kt=kt: e.matmul(
                            self.PS[oc % 2][:, 0:512],
                            WSL(oc % 2)[:, kt * 128:kt * 128 + 128], YC(kt),
                            start=(kt == 0), stop=(kt == 7)),
                            waits=[wld, wy] + ([self.now(self.sS)] if kt == 0 else []))
                    wTq = self.now(self.sT)
                    self.war[("FW", oc % 2)] = wTq
                    self.S(lambda e, oc=oc, tc=tc, bcol=bcol, dst=dst: e.activation(
                        out=dst(oc)[:, tc * 512:tc * 512 + 512],
                        in_=self.PS[oc % 2][:, 0:512], func=AF.Identity,
                        bias=self.f(bcol, 0, 128, oc, 1)), waits=[wTq])
            for ktok in range(4):
                kt_g = tc * 4 + ktok
                for half in (1, 0):
                    for kt in range(8):
                        self.T(lambda e, kt=kt, ktok=ktok, half=half: e.matmul(
                            self.PS[3 - half][:, 0:512],
                            YC(kt)[:, ktok * 128:ktok * 128 + 128],
                            WVT[:, kt * 1024 + half * 512:kt * 1024 + half * 512 + 512],
                            start=(kt == 0), stop=(kt == 7)),
                            waits=[wvw, wy] + ([self.now(self.sS), self.now(self.sV)] if kt == 0 else []))

                wTv = self.now(self.sT)
                if getattr(self, 'attn_dbg', 0) == 57 and tc == 0 and ktok == 0:
                    self.V(lambda e: e.tensor_copy(self.hf(2560, 512), self.PS[3][:, 0:512]), waits=[wTv])
                    self.V(lambda e: e.tensor_copy(self.hf(4608, 512), self.PS[2][:, 0:512]))
                    wVd = self.now(self.sV)
                    self.Yd(lambda e: e.dma_start(out=self.dbg["d1"][6][:, 0:512], in_=self.hf(2560, 512)), waits=[wVd])
                    self.Yd(lambda e: e.dma_start(out=self.dbg["d1"][7][:, 0:512], in_=self.hf(4608, 512)), waits=[wVd])
                    self.final_waits.append(self.now(self.dY))
                for half in range(2):
                    self.S(lambda e, kt_g=kt_g, half=half: e.copy(
                        self.B[:, kt_g * 1056:kt_g * 1056 + 1056]
                            .rearrange("p (h c) -> p h c", h=16)[:, half * 8:half * 8 + 8, 0:64],
                        self.PS[3 - half][:, 0:512]), waits=[wTv])
                if getattr(self, 'attn_dbg', 0) == 58 and tc == 0 and ktok == 0:
                    w58 = [self.now(self.sS)]
                    self.Pd(lambda e: e.dma_start(out=self.dbg["d1"][6][:, 0:1056], in_=self.B[:, 0:1056]), waits=w58)
                    self.Pd(lambda e: e.dma_start(out=self.dbg["d1"][5][:, 0:2048], in_=WVT[:, 0:2048]), waits=w58)
                    self.final_waits.append(self.now(self.dP))
            for h in range(NH):
                base = 64 * (h % 2)
                pti = h // 2
                tp = (64, 0) if base else None
                pg = self.PS[5] if h % 2 == 0 else self.PS[4]
                for qt in range(4):
                    self.T(lambda e, h=h, qt=qt, base=base, pti=pti, tp=tp, pg=pg: e.matmul(
                        pg[:, 2 * qt:2 * qt + 2],
                        YC(pti)[base:base + 64, qt * 128:qt * 128 + 128],
                        self.g("gw2", base, base + 64),
                        start=True, stop=True, tile_position=tp),
                        waits=[wy, self.w_gw2] +
                              ([self.now(self.sV), self.now(self.sS)] if qt == 0 else []))
                wTg = self.now(self.sT)
                self.V(lambda e, pg=pg: e.tensor_tensor(
                    out=self.hf(2560, 8), in0=pg[:, 0:8], in1=self.g("gbb", 0, 128, 0, 8),
                    op=ALU.add), waits=[wTg, self.now(self.sS)])
                wV = self.now(self.sV)
                self.S(lambda e: e.activation(out=self.hf(2568, 8), in_=self.hf(2560, 8),
                                              func=AF.Sigmoid), waits=[wV])
                wSg = self.now(self.sS)
                sg = self.hf(2568, 8).rearrange("p (a c) -> p a c", c=2)
                self.V(lambda e, h=h, sg=sg: e.tensor_scalar(
                    out=self.hf(2576, 4), in0=sg[:, :, 1],
                    scalar1=self.f("gruccol", 0, 128, h, 1), scalar2=-1.0,
                    op0=ALU.mult, op1=ALU.add), waits=[wSg])
                self.V(lambda e, sg=sg: e.tensor_tensor(
                    out=self.hf(2580, 4), in0=self.hf(2576, 4), in1=sg[:, :, 0], op=ALU.mult),
                    waits=[self.now(self.sV)])
                self.V(lambda e, h=h, tc=tc: e.tensor_scalar(
                    out=self.f("gates", 0, 128, h * 16 + tc * 4, 4), in0=self.hf(2580, 4),
                    scalar1=2.0, scalar2=None, op0=ALU.add), waits=[self.now(self.sV)])
        wS = self.now(self.sS)
        self.V(lambda e: e.memset(
            self.B[:, 0:16 * 1056].rearrange("p (x c) -> p x c", c=66)[:, :, 64:65],
            1.0), waits=[wS])
        wvones = self.now(self.sV)
        wgates = wvones
        if getattr(self, 'attn_dbg', 0) == 51:
            for pt in range(NPT):
                self.Pd(lambda e, pt=pt: e.dma_start(
                    out=self.dbg["d1"][pt], in_=QT(pt)), waits=[wvones])
            self.final_waits.append(self.now(self.dP))
            return
        if getattr(self, 'attn_dbg', 0) == 52:
            for pt in range(NPT):
                self.Pd(lambda e, pt=pt: e.dma_start(
                    out=self.dbg["d1"][pt], in_=KT(pt)), waits=[wvones])
            self.final_waits.append(self.now(self.dP))
            return
        if getattr(self, 'attn_dbg', 0) == 53:
            self.attn_dump53 = True

        for pti in range(8):
            i0 = 1024 + pti * 16
            for which, srcf in ((0, QT), (1, KT)):
                self.S(lambda e, srcf=srcf, pti=pti: e.activation(
                    out=self.HLN[:, 8192:10240], in_=srcf(pti), func=AF.Square),
                    waits=[self.now(self.sV), self.now(self.sT)])
                wS = self.now(self.sS)
                for tc in range(NTC):
                    self.T(lambda e, tc=tc: e.matmul(
                        self.PS[7][0:1, 0:512], g("onescol"),
                        self.HLN[:, 8192 + tc * 512:8192 + tc * 512 + 512],
                        start=True, stop=True),
                        waits=([wS] if tc == 0 else []) + [self.now(self.sV)])
                    wT = self.now(self.sT)
                    self.V(lambda e, tc=tc, which=which, i0=i0: e.tensor_reduce(
                        self.R1[0:1, i0 + which * 8 + tc:i0 + which * 8 + tc + 1],
                        self.PS[7][0:1, 0:512], axis=AX.X, op=ALU.max), waits=[wT])
                self.V(lambda e, which=which, i0=i0: e.tensor_reduce(
                    self.R1[0:1, i0 + which * 8 + 4:i0 + which * 8 + 5],
                    self.R1[0:1, i0 + which * 8:i0 + which * 8 + 4],
                    axis=AX.X, op=ALU.max), waits=[self.now(self.sV)])
            wV = self.now(self.sV)
            self.S(lambda e, i0=i0: e.sqrt(self.R1[0:1, i0 + 5:i0 + 6], self.R1[0:1, i0 + 4:i0 + 5]),
                   waits=[wV])
            self.S(lambda e, i0=i0: e.sqrt(self.R1[0:1, i0 + 13:i0 + 14], self.R1[0:1, i0 + 12:i0 + 13]))
            wS = self.now(self.sS)
            self.V(lambda e, i0=i0: e.tensor_tensor(
                out=self.R1[0:1, i0 + 6:i0 + 7], in0=self.R1[0:1, i0 + 5:i0 + 6],
                in1=self.R1[0:1, i0 + 13:i0 + 14], op=ALU.mult), waits=[wS])
            self.V(lambda e, i0=i0: e.tensor_scalar(
                out=self.R1[0:1, i0 + 7:i0 + 8], in0=self.R1[0:1, i0 + 6:i0 + 7],
                scalar1=0.125, scalar2=None, op0=ALU.mult), waits=[self.now(self.sV)])
            self.V(lambda e, i0=i0: e.tensor_tensor(
                out=self.R1[0:1, i0 + 14:i0 + 15], in0=self.R1[0:1, i0 + 7:i0 + 8],
                in1=self.SC[0:1, 6:7], op=ALU.add),
                   waits=[self.w_bterm, self.now(self.sV)])
            self.V(lambda e, i0=i0, pti=pti: e.tensor_scalar(
                out=self.R1b[0:1, 50 + pti:51 + pti], in0=self.R1[0:1, i0 + 14:i0 + 15],
                scalar1=-1.0, scalar2=None, op0=ALU.mult), waits=[self.now(self.sV)])
            wV2 = self.now(self.sV)
            self.T(lambda e, pti=pti: e.matmul(self.PS[4][:, 20:21], self.g("ones1", 0, 1),
                                               self.R1b[0:1, 50 + pti:51 + pti],
                                               start=True, stop=True),
                   waits=[wV2, self.now(self.sS)])
            wT = self.now(self.sT)
            self.S(lambda e, pti=pti: e.copy(self.f("signeg", 0, 128, 2 + pti, 1),
                                             self.PS[4][:, 20:21]), waits=[wT])

        if getattr(self, 'attn_dbg', 0) == 56:
            wpre = [self.now(self.sS), self.now(self.sV), self.now(self.sT), self.now(self.dP)]
            self.Pd(lambda e: e.dma_start(out=self.dbg["d1"][6][:, 0:1056], in_=self.B[:, 0:1056]), waits=wpre)
            self.Pd(lambda e: e.dma_start(out=self.dbg["d1"][5][:, 0:2048], in_=WVT[:, 0:2048]), waits=wpre)
            self.final_waits.append(self.now(self.dP))
            return
        if getattr(self, 'attn_dump53', False):
            wpre = [self.now(self.sS), self.now(self.sV), self.now(self.sT)]
            self.Yd(lambda e: e.dma_start(
                out=self.dbg["d1"][0][:, 0:256], in_=self.f("gates")), waits=wpre)
            self.Yd(lambda e: e.dma_start(
                out=self.dbg["d1"][0][:, 256:268], in_=self.f("signeg")), waits=wpre)
            self.final_waits.append(self.now(self.dY))
            return
        GATED = lambda qt: self.C[:, 4096 + qt * 2048:4096 + qt * 2048 + 2048]
        STRIP = self.C[:, 12288:12288 + 3968]
        CTX = lambda pt: self.C[:, pt * 512:(pt + 1) * 512]
        RB = self.HLN[0:64, 8192:8704]
        for qc in range(NTC):
            for h in range(NH):
                base = 64 * (h % 2)
                pti = h // 2
                tp = (64, 0) if base else None
                widx = self.Pd(lambda e, h=h: e.dma_start(
                    out=self.HLN[:, 10240:10242].bitcast(U32),
                    in_=self.c_stripidx[h]), waits=[self.now(self.dP)])
                wstrip = self.Pd(lambda e: e.indirect_dma_start(
                    out=STRIP, out_offset=None, in_=self.dv_dram,
                    in_offset=IndirectOffsetOnAxis(
                        ap=self.HLN[:, 10240:10242].bitcast(U32), axis=0)),
                    waits=[widx, self.w_dv, self.now(self.sV), self.now(self.sT)])
                wg = []
                for qt in range(4):
                    qtg = qc * 4 + qt
                    w = self.V(lambda e, qt=qt, qtg=qtg, h=h: e.tensor_scalar(
                        out=GATED(qt),
                        in0=STRIP[:, (15 - qtg) * 128:(15 - qtg) * 128 + 2048],
                        scalar1=self.f("gates", 0, 128, h * 16 + qtg, 1),
                        scalar2=QK_SCALE, op0=ALU.mult, op1=ALU.mult),
                        waits=[wstrip, wgates, self.now(self.sT)])
                    wg.append(w)
                sneg = self.f("signeg", 0, 128, 2 + pti, 1)
                wexp_prev = {}
                for pr in range(8):
                    prb = pr % 2
                    PT = self.PSW if prb == 0 else self.PSX
                    base_ps = 0 if prb == 0 else 6
                    wW = ([wexp_prev[prb]] if prb in wexp_prev else [self.now(self.sS)]) + wg
                    for half in range(2):
                        kt = 2 * pr + half
                        ksl = KT(pti)[base:base + 64, kt * 128:kt * 128 + 128]
                        qsl = QT(pti)[base:base + 64, qc * 512:qc * 512 + 512]
                        self.T(lambda e, ksl=ksl, qsl=qsl, base_ps=base_ps, half=half, tp=tp: e.matmul(
                            self.PS[base_ps + half][:, 0:512], ksl, qsl,
                            start=True, stop=False, tile_position=tp),
                            waits=wW if half == 0 else ())
                        for qt in range(4):
                            self.T(lambda e, base_ps=base_ps, half=half, qt=qt, kt=kt, st=(qt == 3): e.matmul(
                                self.PS[base_ps + half][:, qt * 128:qt * 128 + 128],
                                GATED(qt)[:, kt * 128:kt * 128 + 128], g("ident"),
                                start=False, stop=st))
                    wT = self.now(self.sT)
                    self.S(lambda e, PT=PT, prb=prb, sneg=sneg: e.activation(
                        out=self.EXPB[:, prb * 1024:prb * 1024 + 1024],
                        in_=PT[:, 0:1024], func=AF.Exp,
                        bias=sneg, scale=0.125), waits=[wT])
                    wexp = self.now(self.sS)
                    wexp_prev[prb] = wexp
                    for half in range(2):
                        kt = 2 * pr + half
                        self.T(lambda e, kt=kt, h=h, prb=prb, half=half: e.matmul(
                            self.PS[2][0:65, 0:512],
                            self.B[:, kt * 1056 + h * 66:kt * 1056 + h * 66 + 65],
                            self.EXPB[:, prb * 1024 + half * 512:prb * 1024 + half * 512 + 512],
                            start=(kt == 0), stop=(kt == 15)),
                            waits=[wexp, wvones] + ([self.now(self.sV)] if kt == 0 else []))
                wTpv = self.now(self.sT)
                self.V(lambda e: e.tensor_scalar(
                    out=self.R2[0:1, 1024:1536], in0=self.PS[2][64:65, 0:512],
                    scalar1=1e-30, scalar2=None, op0=ALU.add), waits=[wTpv])
                wVd_ = self.now(self.sV)
                self.S(lambda e: e.activation(out=self.R2[0:1, 0:512], in_=self.R2[0:1, 1024:1536],
                                              func=AF.Ln), waits=[wVd_])
                self.S(lambda e: e.activation(out=self.R2b[0:1, 0:512], in_=self.R2[0:1, 0:512],
                                              func=AF.Exp, scale=-1.0))
                wV = self.now(self.sS)
                self.T(lambda e: e.matmul(self.PS[4][0:64, 0:512], g("pat2", 0, 2)[0:1, 0:64],
                                          self.R2b[0:1, 0:512], start=True, stop=True),
                       waits=[wV, self.now(self.sS)])
                wT2 = self.now(self.sT)
                self.S(lambda e: e.copy(RB, self.PS[4][0:64, 0:512]), waits=[wT2])
                wS3 = self.now(self.sS)
                if h % 2 == 0:
                    self.V(lambda e, pti=pti: e.tensor_tensor(
                        out=CTX(pti)[0:64, :], in0=self.PS[2][0:64, 0:512], in1=RB,
                        op=ALU.mult), waits=[wS3, self.now(self.sT)])
                else:
                    self.V(lambda e: e.tensor_tensor(
                        out=self.ODD[0:64, 0:512], in0=self.PS[2][0:64, 0:512], in1=RB,
                        op=ALU.mult), waits=[wS3, self.now(self.dP), self.now(self.sT)])
                    wV2 = self.now(self.sV)
                    self.Pd(lambda e, pti=pti: e.dma_start(
                        out=CTX(pti)[64:128, :], in_=self.ODD[0:64, 0:512]), waits=[wV2])
            if getattr(self, 'attn_dbg', 0) == 55 and qc == 0:
                wpre = [self.now(self.sS), self.now(self.sV), self.now(self.sT), self.now(self.dP)]
                self.Pd(lambda e: e.dma_start(out=self.dbg["d1"][6][:, 0:1056], in_=self.B[:, 0:1056]), waits=wpre)
                self.Pd(lambda e: e.dma_start(out=self.dbg["d1"][7][:, 0:2048], in_=self.Abf[:, 0:2048] if hasattr(self, 'Abf') else self.A[:, :].bitcast(BF16)[:, 0:2048]), waits=wpre)
                self.Pd(lambda e: e.dma_start(out=self.dbg["d1"][0][:, 0:2048], in_=STRIP[:, 0:2048]), waits=wpre)
                self.Pd(lambda e: e.dma_start(out=self.dbg["d1"][4][:, 0:1920], in_=STRIP[:, 2048:3968]), waits=wpre)
                self.Pd(lambda e: e.dma_start(out=self.dbg["d1"][1][:, 0:2048], in_=GATED(0)), waits=wpre)
                self.Pd(lambda e: e.dma_start(out=self.dbg["d1"][2][:, 0:2048], in_=self.EXPB[:, :]), waits=wpre)
                self.Pd(lambda e: e.dma_start(out=self.dbg["d1"][3][:, 0:2048], in_=self.C[:, 0:2048]), waits=wpre)
                self.Pd(lambda e: e.dma_start(out=self.dbg["d1"][5][:, 0:2048], in_=self.C[:, 2048:4096]), waits=wpre)
                self.final_waits.append(self.now(self.dP))
                return
            wodd = self.now(self.dP)
            wctx = self.now(self.sV)
            XC = self.hf(5632, 512)
            RC = self.hf(6144, 512)
            for oc in range(8):
                wld = self.wload(("FW", oc % 2), "attnout", lambda e, oc=oc: e.dma_start(
                    out=WSL(oc % 2), in_=self.Wslab["Wo"][oc]))
                for kt in range(8):
                    self.T(lambda e, oc=oc, kt=kt: e.matmul(
                        self.PS[5 + (oc % 2)][:, 0:512],
                        WSL(oc % 2)[:, kt * 128:kt * 128 + 128],
                        CTX(kt), start=(kt == 0), stop=(kt == 7)),
                        waits=[wld, wodd, wctx] + ([self.now(self.sV)] if kt == 0 else []))
                wT = self.now(self.sT)
                self.war[("FW", oc % 2)] = wT
                wx = self.Xd(lambda e, oc=oc, qc=qc, b=b: e.dma_start(
                    out=XC, in_=self.x_state[b][oc, :, qc * 512:qc * 512 + 512]),
                    waits=[self.now(self.sV)])
                self.V(lambda e, oc=oc: e.scalar_tensor_tensor(
                    out=RC, in0=self.PS[5 + (oc % 2)][:, 0:512],
                    scalar=self.f("bo2col", 0, 128, oc, 1),
                    in1=XC, op0=ALU.add, op1=ALU.add), waits=[wT, wx])
                wV = self.now(self.sV)
                self.Yd(lambda e, oc=oc, qc=qc, b=b: e.dma_start(
                    out=self.x_state[b][oc, :, qc * 512:qc * 512 + 512], in_=RC),
                    waits=[wV])
        self.final_waits.append(self.now(self.dY))


def build_program(debug_stage=0):
    nc = bass.Bass("TRN2", target_bir_lowering=False)
    bd = BD(nc, debug_stage)
    ctx = ExitStack()
    with ctx:
        bd.alloc(ctx)
        bd.mk_sems()
        bd.body()
        for w in bd.final_waits:
            bd.sch.op("sync", None, [w], ())
        blk = ctx.enter_context(nc.Block())
        bd.sch.emit(blk)
    return nc


def kernel(**inputs):
    d = prep_host(inputs)
    nc = build_program(DEBUG_STAGE)
    xin_all = d.pop("xin")
    in_maps = []
    for core in range(8):
        m = dict(d)
        m["xin"] = np.ascontiguousarray(xin_all[2 * core:2 * core + 2])
        in_maps.append(m)
    res = run_bass_kernel_spmd(nc, in_maps, core_ids=list(range(8)))
    outs = [np.asarray(res.results[c]["xout"]).reshape(2, E, N) for c in range(8)]
    full = np.concatenate(outs, axis=0)
    return np.ascontiguousarray(np.transpose(full, (0, 2, 1)))


# version 07 (final)
# speedup vs baseline: 1.0576x; 1.0576x over previous
import math
import numpy as np
import ml_dtypes
from contextlib import ExitStack

import concourse.bass as bass
import concourse.mybir as mybir
from concourse.bass_utils import run_bass_kernel_spmd
from concourse.ap import AP
from bass_rust import IndirectOffsetOnAxis

F32 = mybir.dt.float32
BF16 = mybir.dt.bfloat16
U32 = mybir.dt.uint32
AF = mybir.ActivationFunctionType
ALU = mybir.AluOpType
AX = mybir.AxisListType

E = 1024
N = 2048
NH = 16
HD = 64
LH = 8
WIN = 512
HID = 2730
QK_SCALE = 8.0
NBUCK = 320
MAXDIST = 800
MASKVAL = -30000.0

NPT = 8
NTC = 4
TC = 512
ZW = 2052

DEBUG_STAGE = 0


def _bucket_of_rel():
    rel = np.arange(-2047, 2048)
    nb = NBUCK // 2
    buckets = (rel > 0).astype(np.int64) * nb
    rel_abs = np.abs(rel)
    max_exact = nb // 2
    is_small = rel_abs < max_exact
    large = np.log(np.maximum(rel_abs, 1).astype(np.float64) / max_exact) / \
        math.log(MAXDIST / max_exact) * (nb - max_exact)
    large = (max_exact + large).astype(np.int64)
    large = np.minimum(large, nb - 1)
    return buckets + np.where(is_small, rel_abs, large)

def make_consts():
    c = {}
    bk = _bucket_of_rel()
    S = np.zeros((NBUCK, 4095), np.float32)
    S[bk, np.arange(4095)] = 1.0
    c["c_S"] = S
    idx = np.zeros((NH, 128, 1), np.uint32)
    for h in range(NH):
        idx[h, :, 0] = h * 4224 + 127 - np.arange(128)
    c["c_stripidx"] = idx
    c["c_ident"] = np.eye(128, dtype=np.float32)
    q_ = np.arange(128)[:, None]
    k_ = np.arange(128)[None, :]
    c["c_mask0"] = np.where(k_ - q_ > 0, MASKVAL, 0.0).astype(np.float32)
    c["c_mask4"] = np.where(k_ - q_ <= 0, MASKVAL, 0.0).astype(np.float32)
    R = np.zeros((64, 64), np.float32)
    for do_ in range(32):
        R[do_ + 32, do_] = -1.0
    for do_ in range(32, 64):
        R[do_ - 32, do_] = 1.0
    R128 = np.zeros((128, 128), np.float32)
    R128[0:64, 0:64] = R
    R128[64:128, 64:128] = R
    c["c_rot"] = R128
    ones2 = np.zeros((128, 2), np.float32)
    ones2[0:64, 0] = 1.0
    ones2[64:128, 1] = 1.0
    c["c_ones2"] = ones2
    pat2 = np.zeros((2, 128), np.float32)
    pat2[0, 0:64] = 1.0
    pat2[1, 64:128] = 1.0
    c["c_pat2"] = pat2
    c["c_ones1"] = np.ones((1, 128), np.float32)
    c["c_onescol"] = np.ones((128, 1), np.float32)
    P8 = np.zeros((8, 2), np.float32)
    P8[0:4, 0] = 1.0
    P8[4:8, 1] = 1.0
    c["c_P8"] = P8
    hi = np.arange(0, HD, 2, dtype=np.float64)
    inv = 1.0 / 10000.0 ** (hi / HD)
    t = np.arange(N, dtype=np.float64)
    fr = np.concatenate([t[:, None] * inv[None, :]] * 2, axis=1)
    base = (hi + 0.4 * HD) / (1.4 * HD)
    power = (t - N // 2) / float(WIN // 2)
    sc = np.concatenate([base[None, :] ** power[:, None]] * 2, axis=1)
    dup = lambda a: np.concatenate([a, a], axis=0).astype(np.float32)
    c["c_cs"] = dup((np.cos(fr) * sc).T)
    c["c_ss"] = dup((np.sin(fr) * sc).T)
    c["c_ci"] = dup((np.cos(fr) / sc).T)
    c["c_si"] = dup((np.sin(fr) / sc).T)
    return c

def prep_host(inputs):
    g = inputs
    d = {}
    d["xin"] = np.ascontiguousarray(
        np.transpose(g["L0_attn_hidden_states"], (0, 2, 1))).reshape(16, 8, 128, N)
    for nm in ("Wq", "Wk", "Wo"):
        WT = g["L0_attn_" + nm].T
        d[nm + "slab"] = np.ascontiguousarray(
            WT.reshape(8, 128, 8, 128).transpose(2, 1, 0, 3).reshape(8, 128, 1024))
    d["WvTr"] = np.ascontiguousarray(
        g["L0_attn_Wv"].T.reshape(8, 128, 1024).transpose(1, 0, 2).reshape(128, 8192))
    d["WoT"] = np.ascontiguousarray(g["L0_attn_Wo"].T)
    for nm in ("bq", "bk", "bv", "bo"):
        d[nm] = np.ascontiguousarray(g["L0_attn_" + nm])
    d["gruW"] = np.ascontiguousarray(g["L0_attn_gru_W"])
    d["grub"] = np.ascontiguousarray(g["L0_attn_gru_b"].reshape(8, 1))
    d["gruc"] = np.ascontiguousarray(g["L0_attn_gru_const"].reshape(1, NH))
    d["rel"] = np.ascontiguousarray(g["L0_attn_rel_embed"])
    wqkv = g["L0_ffn_wqkv"]
    d["wqkv_slab"] = np.ascontiguousarray(
        wqkv.reshape(2, 8, 128, 12, 128).transpose(0, 3, 2, 1, 4).reshape(2, 12, 128, 1024)[:, 0:8])
    d["wqkv_vw"] = np.ascontiguousarray(
        wqkv.reshape(2, 8, 128, 1536)[:, :, :, 1024:].transpose(0, 2, 1, 3).reshape(2, 128, 4096))
    d["wo_slab"] = np.ascontiguousarray(
        g["L0_ffn_wo"].reshape(2, 4, 128, 8, 128).transpose(0, 3, 2, 1, 4).reshape(2, 8, 128, 512))
    w1a = np.zeros((2, 1024, 2816), np.float32); w1a[:, :, :HID] = g["L0_ffn_w1"][:, :, :HID]
    w1g = np.zeros((2, 1024, 2816), np.float32); w1g[:, :, :HID] = g["L0_ffn_w1"][:, :, HID:]
    d["w1a_slab"] = np.ascontiguousarray(
        w1a.reshape(2, 8, 128, 22, 128).transpose(0, 3, 2, 1, 4).reshape(2, 22, 128, 1024))
    d["w1g_slab"] = np.ascontiguousarray(
        w1g.reshape(2, 8, 128, 22, 128).transpose(0, 3, 2, 1, 4).reshape(2, 22, 128, 1024))
    b1ap = np.zeros((2, 2816), np.float32); b1ap[:, :HID] = g["L0_ffn_b1"][:, :HID]
    b1gp = np.zeros((2, 2816), np.float32); b1gp[:, :HID] = g["L0_ffn_b1"][:, HID:]
    d["b1a"] = b1ap
    d["b1g"] = b1gp
    w2p = np.zeros((2, 2816, 1024), np.float32); w2p[:, :HID] = g["L0_ffn_w2"]
    d["w2t"] = np.ascontiguousarray(
        w2p.reshape(2, 22, 128, 8, 128).transpose(0, 3, 2, 1, 4).reshape(2, 8, 128, 2816))
    d["b2"] = np.ascontiguousarray(g["L0_ffn_b2"])
    d["lng"] = np.ascontiguousarray(g["L0_ffn_ln_g"])
    d["lnb"] = np.ascontiguousarray(g["L0_ffn_ln_b"])
    qs = g["L0_ffn_q_scale"]; ks = g["L0_ffn_k_scale"]
    colpair = lambda v: np.ascontiguousarray(np.concatenate([v, v], axis=1)[:, :, None])
    d["qscol"] = colpair(qs)
    d["kscol"] = colpair(ks)
    d["qsrot"] = colpair(np.roll(qs, -32, axis=1))
    d["ksrot"] = colpair(np.roll(ks, -32, axis=1))
    d["qsrow"] = np.ascontiguousarray(qs.reshape(2, 1, 64))
    d["ksrow"] = np.ascontiguousarray(ks.reshape(2, 1, 64))
    d.update(make_consts())
    return d


class Sem:
    __slots__ = ("h", "n", "name")
    def __init__(self, nc, name):
        self.h = nc.alloc_semaphore(name=name)
        self.n = 0
        self.name = name

class Sched:
    ENGS = ("tensor", "vector", "scalar", "gpsimd", "sync")
    def __init__(self):
        self.q = {e: [] for e in self.ENGS}

    def op(self, eng, fn, waits=(), incs=()):
        ws = tuple((s, v) for (s, v) in waits if v > 0)
        for s, k in incs:
            s.n += k
        self.q[eng].append((ws, fn, tuple(incs)))
        if incs:
            s0 = incs[0][0]
            return (s0, s0.n)
        return None

    def emit(self, block):
        for ename in self.ENGS:
            items = self.q[ename]
            def make(items):
                def _(eng):
                    last = {}
                    for waits, fn, incs in items:
                        for s, v in waits:
                            if last.get(s.name, -1) < v:
                                eng.wait_ge(s.h, v)
                                last[s.name] = v
                        if fn is not None:
                            inst = fn(eng)
                            for s, k in incs:
                                inst.then_inc(s.h, k)
                return _
            getattr(block, ename)(make(items))


class BD:
    def __init__(self, nc, debug_stage=0):
        self.nc = nc
        self.sch = Sched()
        self.debug_stage = debug_stage
        self.final_waits = []
        self._mk_io()

    def _mk_io(self):
        nc = self.nc
        def inp(name, shape, dt=F32):
            return nc.dram_tensor(name, shape, dt, kind="ExternalInput").ap()
        self.xin = inp("xin", [2, 8, 128, N])
        self.Wslab = {nm: inp(nm + "slab", [8, 128, 1024]) for nm in ("Wq", "Wk", "Wo")}
        self.WvTr = inp("WvTr", [128, 8192])
        self.WoT = inp("WoT", [E, E])
        self.battn = {nm: inp(nm, [E]) for nm in ("bq", "bk", "bv", "bo")}
        self.gruW = inp("gruW", [8, 64])
        self.grub = inp("grub", [8, 1])
        self.gruc = inp("gruc", [1, NH])
        self.rel = inp("rel", [NBUCK, NH])
        self.wqkv_slab = inp("wqkv_slab", [2, 8, 128, 1024])
        self.wqkv_vw = inp("wqkv_vw", [2, 128, 4096])
        self.wo_slab = inp("wo_slab", [2, 8, 128, 512])
        self.w1a_slab = inp("w1a_slab", [2, 22, 128, 1024])
        self.w1g_slab = inp("w1g_slab", [2, 22, 128, 1024])
        self.b1a = inp("b1a", [2, 2816])
        self.b1g = inp("b1g", [2, 2816])
        self.w2t = inp("w2t", [2, 8, 128, 2816])
        self.b2 = inp("b2", [2, E])
        self.lng = inp("lng", [2, E])
        self.lnb = inp("lnb", [2, E])
        self.qscol = inp("qscol", [2, 128, 1]); self.kscol = inp("kscol", [2, 128, 1])
        self.qsrot = inp("qsrot", [2, 128, 1]); self.ksrot = inp("ksrot", [2, 128, 1])
        self.qsrow = inp("qsrow", [2, 1, 64]); self.ksrow = inp("ksrow", [2, 1, 64])
        self.c_S = inp("c_S", [NBUCK, 4095])
        self.c_stripidx = inp("c_stripidx", [NH, 128, 1], U32)
        self.c_ident = inp("c_ident", [128, 128])
        self.c_mask0 = inp("c_mask0", [128, 128])
        self.c_mask4 = inp("c_mask4", [128, 128])
        self.c_rot = inp("c_rot", [128, 128])
        self.c_ones2 = inp("c_ones2", [128, 2])
        self.c_pat2 = inp("c_pat2", [2, 128])
        self.c_ones1 = inp("c_ones1", [1, 128])
        self.c_onescol = inp("c_onescol", [128, 1])
        self.c_P8 = inp("c_P8", [8, 2])
        self.c_cs = inp("c_cs", [128, N]); self.c_ss = inp("c_ss", [128, N])
        self.c_ci = inp("c_ci", [128, N]); self.c_si = inp("c_si", [128, N])
        self.xout = nc.dram_tensor("xout", [2, 8, 128, N], F32, kind="ExternalOutput").ap()
        self.x_state = nc.dram_tensor("x_state", [2, 8, 128, N], F32, kind="Internal").ap()
        self.dv_dram = nc.dram_tensor("dv_dram", [NH * 4224, 1], BF16, kind="Internal").ap()
        self.dbg = {}
        if self.debug_stage:
            self.dbg["d1"] = nc.dram_tensor("dbg1", [8, 128, N], F32, kind="ExternalOutput").ap()

    def alloc(self, ctx):
        EN = ctx.enter_context
        nc = self.nc
        self.A = EN(nc.sbuf_tensor("arA", [128, 8 * ZW], F32))
        self.B = EN(nc.sbuf_tensor("arB", [128, 16896], BF16))
        self.C = EN(nc.sbuf_tensor("arC", [128, 16384], BF16))
        self.HLN = EN(nc.sbuf_tensor("arH", [128, 16384], BF16))
        self.Fw = EN(nc.sbuf_tensor("arF", [128, 6656], BF16))
        self.EXPB = EN(nc.sbuf_tensor("arE", [128, 2048], BF16))
        self.G = EN(nc.sbuf_tensor("arG", [128, 1856], BF16))
        self.Gf = EN(nc.sbuf_tensor("arGf", [128, 640], F32))
        self.R2 = EN(nc.sbuf_tensor("arR2", [2, 1536], F32))
        self.SC = EN(nc.sbuf_tensor("arSC", [1, 64], F32))
        self.EPS2 = EN(nc.sbuf_tensor("arEPS2", [2, 1], F32))
        self.R2b = EN(nc.sbuf_tensor("arR2b", [2, 512], BF16))
        self.R1 = EN(nc.sbuf_tensor("arR1", [1, 3072], F32))
        self.R1b = EN(nc.sbuf_tensor("arR1b", [1, 1024], BF16))
        self.ODD = EN(nc.sbuf_tensor("arODD", [64, 512], BF16))
        self.DVS = EN(nc.sbuf_tensor("arDVS", [16, 512], BF16))
        self.PSW = EN(nc.psum_tensor("PSW", [128, 1024], F32))
        self.PSX = EN(nc.psum_tensor("PSX", [128, 1024], F32))
        _ps2 = [EN(nc.psum_tensor(f"PS{i}", [128, 512], F32)) for i in range(2, 6)]
        class _PV:
            def __init__(s, ten, off):
                s.t = ten; s.off = off
            def __getitem__(s, key):
                p, c = key
                c0 = c.start if c.start is not None else 0
                c1 = c.stop if c.stop is not None else 512
                return s.t[p, s.off + c0:s.off + c1]
        self.PS = [_PV(self.PSW, 0), _PV(self.PSW, 512), _ps2[0], _ps2[1],
                   _ps2[2], _ps2[3], _PV(self.PSX, 0), _PV(self.PSX, 512)]

        self.g_, p = {}, 0
        for nm, w in [("ident", 128), ("mask0", 128), ("mask4", 128), ("rot", 128),
                      ("ones2", 2), ("pat2", 128), ("ones1", 128), ("onescol", 1),
                      ("P8", 2), ("gw2", 2), ("gbb", 32), ("ab", 1024)]:
            self.g_[nm] = (p, w); p += w
        assert p <= 1856, p
        self.f_, p = {}, 0
        for nm, w in [("bqcol", 8), ("bkcol", 8), ("bvcol", 8), ("bocol", 8),
                      ("b2col", 16), ("lngcol", 16), ("lnbcol", 16),
                      ("b1acol", 44), ("b1gcol", 44),
                      ("qscol", 2), ("kscol", 2), ("qsrot", 2), ("ksrot", 2),
                      ("gruccol", 16), ("signeg", 12), ("gates", 256),
                      ("bo2col", 8), ("ones1f", 128)]:
            self.f_[nm] = (p, w); p += w
        assert p <= 640, p

    def g(self, nm, p0=0, p1=128, j=0, w=None):
        c0, wd = self.g_[nm]
        if w is None:
            w = wd - j
        return self.G[p0:p1, c0 + j:c0 + j + w]

    def f(self, nm, p0=0, p1=128, j=0, w=None):
        c0, wd = self.f_[nm]
        if w is None:
            w = wd - j
        return self.Gf[p0:p1, c0 + j:c0 + j + w]

    def mk_sems(self):
        self.war = {}
        self.region_role = {}
        nc = self.nc
        self.sT = Sem(nc, "sT"); self.sV = Sem(nc, "sV"); self.sS = Sem(nc, "sS")
        self.dP = Sem(nc, "dP"); self.dV = Sem(nc, "dV")
        self.dX = Sem(nc, "dX"); self.dY = Sem(nc, "dY")
        self.dXr = Sem(nc, "dXr")
        self.dYa = [Sem(nc, f"dYa{i}") for i in range(4)]
        self.dPo = Sem(nc, "dPo")
        self.dPv = Sem(nc, "dPv")
        self.dPk = {}
        self.dPa = [Sem(nc, "dPa0"), Sem(nc, "dPa1")]
        self.dPs = Sem(nc, "dPs"); self.dPi = Sem(nc, "dPi")
        self.dVa = [Sem(nc, "dVa0"), Sem(nc, "dVa1")]
        self.dVc = Sem(nc, "dVc")
        self.dXa = [Sem(nc, f"dXa{i}") for i in range(4)]

    def T(self, fn, waits=()):
        return self.sch.op("tensor", fn, waits, ((self.sT, 1),))
    def V(self, fn, waits=()):
        return self.sch.op("vector", fn, waits, ((self.sV, 1),))
    def S(self, fn, waits=()):
        return self.sch.op("scalar", fn, waits, ((self.sS, 1),))
    def Pd(self, fn, waits=()):
        return self.sch.op("gpsimd", fn, waits, ((self.dP, 16),))

    def wload(self, key, role, fn, extra=()):
        if self.region_role.get(key) != role:
            waits = [self.now(s) for s in
                     (self.sT, self.sS, self.sV, self.dY, self.dP, self.dPi,
                      self.dPs, self.dPo, self.dPv, self.dV, self.dVc,
                      self.dX, self.dXr, *self.dYa, *self.dXa, *self.dVa,
                      *self.dPa, *self.dPk.values())]
            self.region_role[key] = role
        else:
            waits = [self.war[key]] if key in self.war else []
        if isinstance(key, tuple):
            sem = self.dPa[key[1] % 2]
        else:
            if key not in self.dPk:
                self.dPk[key] = Sem(self.nc, f"dPk{len(self.dPk)}")
            sem = self.dPk[key]
        return self.sch.op("gpsimd", fn, tuple(list(waits) + [w for w in extra if w is not None]),
                           ((sem, 16),))
    def Vd(self, fn, waits=()):
        return self.sch.op("gpsimd", fn, waits, ((self.dV, 16),))
    def Xd(self, fn, waits=()):
        return self.sch.op("scalar", fn, waits, ((self.dX, 16),))
    def Yd(self, fn, waits=()):
        return self.sch.op("sync", fn, waits, ((self.dY, 16),))
    def now(self, sem):
        return (sem, sem.n)
    def barrierTVS(self):
        return [self.now(self.sT), self.now(self.sV), self.now(self.sS)]

    def hf(self, c0, w):
        return self.HLN[:, :].bitcast(F32)[:, c0:c0 + w]

    def zsl(self, pt, t0, w):
        return self.A[:, pt * ZW + 1 + t0: pt * ZW + 1 + t0 + w]

    def ph0(self):
        g = self.g; f = self.f
        self.V(lambda e: e.memset(self.SC[0:1, 8:9], 1e-5))
        self.V(lambda e: e.memset(self.f("ones1f", 0, 1), 1.0))
        self.V(lambda e: e.memset(self.EPS2[0:2, 0:1], 1e-24))
        self.w_eps = self.now(self.sV)
        for nm, src, p1 in [("ident", self.c_ident, 128), ("mask0", self.c_mask0, 128),
                            ("mask4", self.c_mask4, 128), ("rot", self.c_rot, 128),
                            ("ones2", self.c_ones2, 128), ("pat2", self.c_pat2, 2),
                            ("ones1", self.c_ones1, 1), ("onescol", self.c_onescol, 128),
                            ("P8", self.c_P8, 8)]:
            self.Pd(lambda e, nm=nm, src=src, p1=p1: e.dma_start(out=self.g(nm, 0, p1), in_=src))
        w_const = self.now(self.dP)

        colv = lambda ap: ap.rearrange("(a b) -> a b", b=1)
        for pt in range(8):
            for nm, src in [("bqcol", self.battn["bq"]), ("bkcol", self.battn["bk"]),
                            ("bvcol", self.battn["bv"]), ("bocol", self.battn["bo"])]:
                self.Xd(lambda e, nm=nm, src=src, pt=pt: e.dma_start(
                    out=self.f(nm, 0, 128, pt, 1), in_=colv(src[pt * 128:(pt + 1) * 128])))
        for l in range(2):
            for pt in range(8):
                for nm, src in [("b2col", self.b2), ("lngcol", self.lng), ("lnbcol", self.lnb)]:
                    self.Xd(lambda e, nm=nm, src=src, l=l, pt=pt: e.dma_start(
                        out=self.f(nm, 0, 128, l * 8 + pt, 1),
                        in_=colv(src[l, pt * 128:(pt + 1) * 128])))
            for pt in range(22):
                for nm, src in [("b1acol", self.b1a), ("b1gcol", self.b1g)]:
                    self.Xd(lambda e, nm=nm, src=src, l=l, pt=pt: e.dma_start(
                        out=self.f(nm, 0, 128, l * 22 + pt, 1),
                        in_=colv(src[l, pt * 128:pt * 128 + 128])))
            for nm, src in [("qscol", self.qscol), ("kscol", self.kscol),
                            ("qsrot", self.qsrot), ("ksrot", self.ksrot)]:
                self.Xd(lambda e, nm=nm, src=src, l=l: e.dma_start(
                    out=self.f(nm, 0, 128, l, 1), in_=src[l]))
        w_cols = self.now(self.dX)

        for kt in range(3):
            h0 = kt * 128; h1 = min(NBUCK, h0 + 128)
            self.Pd(lambda e, kt=kt, h0=h0, h1=h1: e.dma_start(
                out=self.Fw[0:h1 - h0, kt * 16:kt * 16 + 16], in_=self.rel[h0:h1, :]))
        w_rel = self.now(self.dP)
        for ch in range(8):
            c0 = ch * 512; c1 = min(4095, c0 + 512); w = c1 - c0
            wS_ev = self.now(self.sS)
            for kt in range(3):
                h0 = kt * 128; h1 = min(NBUCK, h0 + 128)
                wld = self.Pd(lambda e, kt=kt, h0=h0, h1=h1, c0=c0, w=w, ch=ch: e.dma_start(
                    out=self.Fw[0:h1 - h0, 64 + (ch % 2) * 512:64 + (ch % 2) * 512 + w],
                    in_=self.c_S[h0:h1, c0:c0 + w]), waits=[self.now(self.sT)])
                self.T(lambda e, kt=kt, h0=h0, h1=h1, w=w, ch=ch: e.matmul(
                    self.PS[ch % 2][0:16, 0:w],
                    self.Fw[0:h1 - h0, kt * 16:kt * 16 + 16],
                    self.Fw[0:h1 - h0, 64 + (ch % 2) * 512:64 + (ch % 2) * 512 + w],
                    start=(kt == 0), stop=(kt == 2)),
                    waits=[wld, w_rel, wS_ev])
            wT = self.now(self.sT)
            self.S(lambda e, w=w, ch=ch: e.copy(self.DVS[:, 0:w], self.PS[ch % 2][0:16, 0:w]),
                   waits=[wT, self.now(self.dPv)])
            wS2 = self.now(self.sS)
            self.sch.op("gpsimd", lambda e, c0=c0, w=w: e.dma_start(
                out=AP(self.dv_dram.tensor, c0, [[4224, 16], [1, w]]),
                in_=self.DVS[:, 0:w]), (wS2,), ((self.dPv, 16),))
        self.V(lambda e: e.memset(self.DVS[:, 0:129], 0.0), waits=[self.now(self.dPv)])
        self.sch.op("gpsimd", lambda e: e.dma_start(
            out=AP(self.dv_dram.tensor, 4095, [[4224, 16], [1, 129]]),
            in_=self.DVS[:, 0:129]), (self.now(self.sV),), ((self.dPv, 16),))
        self.w_dv = self.now(self.dPv)

        self.Pd(lambda e: e.dma_start(out=self.Fw[0:8, 0:64], in_=self.gruW))
        self.Pd(lambda e: e.dma_start(out=self.Fw[0:8, 64:65], in_=self.grub))
        w_gru = self.now(self.dP)
        self.T(lambda e: e.matmul(self.PS[3][0:64, 0:2], self.Fw[0:8, 0:64],
                                  self.g("P8", 0, 8), start=True, stop=True),
               waits=[w_gru, w_const])
        self.T(lambda e: e.matmul(self.PS[3][0:1, 4:6], self.Fw[0:8, 64:65],
                                  self.g("P8", 0, 8), start=True, stop=True))
        wT = self.now(self.sT)
        self.S(lambda e: e.copy(self.g("gw2", 0, 64), self.PS[3][0:64, 0:2]), waits=[wT])
        self.S(lambda e: e.copy(self.R1b[0:1, 0:2], self.PS[3][0:1, 4:6]))
        wS = self.now(self.sS)
        self.Pd(lambda e: e.dma_start(out=self.g("gw2", 64, 128), in_=self.g("gw2", 0, 64)),
                waits=[wS])
        self.w_gw2 = self.now(self.dP)
        self.T(lambda e: e.matmul(
            self.PS[4][:, 0:32], self.g("ones1", 0, 1),
            self.R1b[0:1, 0:2][:, None, :].broadcast_to([1, 16, 2]),
            start=True, stop=True), waits=[wS])
        wT = self.now(self.sT)
        self.S(lambda e: e.copy(self.g("gbb", 0, 128), self.PS[4][:, 0:32]), waits=[wT])

        self.Xd(lambda e: e.dma_start(out=self.R1[0:1, 0:16], in_=self.gruc))
        w_gc = self.now(self.dX)
        self.V(lambda e: e.tensor_copy(self.R1b[0:1, 16:32], self.R1[0:1, 0:16]), waits=[w_gc])
        wV = self.now(self.sV)
        self.T(lambda e: e.matmul(self.PS[5][:, 0:16], self.g("ones1", 0, 1),
                                  self.R1b[0:1, 16:32], start=True, stop=True),
               waits=[wV, w_const])
        wT = self.now(self.sT)
        self.S(lambda e: e.copy(self.f("gruccol"), self.PS[5][:, 0:16]), waits=[wT])
        self.V(lambda e: e.tensor_reduce(self.SC[0:1, 10:11], self.R1[0:1, 0:16],
                                         axis=AX.X, op=ALU.max, apply_absolute_value=True),
               waits=[w_gc])
        relflat = self.rel.rearrange("a b -> (a b)").rearrange("(a b) -> a b", a=1)
        for ch in range(3):
            c0 = ch * 2048; c1 = min(5120, c0 + 2048)
            w_rl = self.sch.op("scalar", lambda e, c0=c0, c1=c1: e.dma_start(
                out=self.R1[0:1, 0:c1 - c0], in_=relflat[:, c0:c1]),
                (self.now(self.sV),), ((self.dXr, 16),))
            self.V(lambda e, ch=ch, w=c1 - c0: e.tensor_reduce(
                self.SC[0:1, 1 + ch:2 + ch], self.R1[0:1, 0:w],
                axis=AX.X, op=ALU.max, apply_absolute_value=True), waits=[w_rl])
        self.V(lambda e: e.tensor_reduce(self.SC[0:1, 4:5], self.SC[0:1, 1:4],
                                         axis=AX.X, op=ALU.max), waits=[self.now(self.sV)])
        self.V(lambda e: e.tensor_scalar(out=self.SC[0:1, 5:6], in0=self.SC[0:1, 10:11],
                                         scalar1=3.0, scalar2=None, op0=ALU.add),
               waits=[self.now(self.sV)])
        self.V(lambda e: e.tensor_tensor(out=self.SC[0:1, 6:7], in0=self.SC[0:1, 5:6],
                                         in1=self.SC[0:1, 4:5], op=ALU.mult),
               waits=[self.now(self.sV)])
        self.w_bterm = self.now(self.sV)

        for l in range(2):
            self.Xd(lambda e, l=l: e.dma_start(
                out=self.R1[0:1, 2560 + 128 * l:2624 + 128 * l], in_=self.qsrow[l]))
            self.Xd(lambda e, l=l: e.dma_start(
                out=self.R1[0:1, 2624 + 128 * l:2688 + 128 * l], in_=self.ksrow[l]))
        w_qs = self.now(self.dX)
        for l in range(2):
            self.V(lambda e, l=l: e.tensor_reduce(
                self.R1[0:1, 24:25], self.R1[0:1, 2560 + 128 * l:2624 + 128 * l],
                axis=AX.X, op=ALU.max, apply_absolute_value=True), waits=[w_qs])
            self.V(lambda e, l=l: e.tensor_reduce(
                self.R1[0:1, 25:26], self.R1[0:1, 2624 + 128 * l:2688 + 128 * l],
                axis=AX.X, op=ALU.max, apply_absolute_value=True))
            self.V(lambda e: e.tensor_tensor(out=self.R1[0:1, 26:27], in0=self.R1[0:1, 24:25],
                                             in1=self.R1[0:1, 25:26], op=ALU.mult),
                   waits=[self.now(self.sV)])
            self.V(lambda e, l=l: e.tensor_scalar(
                out=self.R1b[0:1, 40 + l:41 + l], in0=self.R1[0:1, 26:27],
                scalar1=-QK_SCALE, scalar2=None, op0=ALU.mult),
                   waits=[self.now(self.sV)])
            wV = self.now(self.sV)
            self.T(lambda e, l=l: e.matmul(self.PS[5][:, 20 + l:21 + l], self.g("ones1", 0, 1),
                                           self.R1b[0:1, 40 + l:41 + l], start=True, stop=True),
                   waits=[wV, self.now(self.sS)])
            wT = self.now(self.sT)
            self.S(lambda e, l=l: e.copy(self.f("signeg", 0, 128, l, 1),
                                         self.PS[5][:, 20 + l:21 + l]), waits=[wT])

        self.V(lambda e: e.tensor_copy(self.EXPB[:, 0:8], self.f("bvcol")), waits=[w_cols])
        w_bvb = self.now(self.sV)
        for oc in range(8):
            for kt in range(8):
                wld = self.Pd(lambda e, oc=oc, kt=kt: e.dma_start(
                    out=self.Fw[:, 1024 + (kt % 2) * 128:1152 + (kt % 2) * 128],
                    in_=self.WoT[kt * 128:(kt + 1) * 128, oc * 128:(oc + 1) * 128]),
                    waits=[self.now(self.sT)])
                self.T(lambda e, oc=oc, kt=kt: e.matmul(
                    self.PS[6][:, oc:oc + 1],
                    self.Fw[:, 1024 + (kt % 2) * 128:1152 + (kt % 2) * 128],
                    self.EXPB[:, kt:kt + 1],
                    start=(kt == 0), stop=(kt == 7)), waits=[wld, w_bvb])
        wT = self.now(self.sT)
        self.V(lambda e: e.tensor_tensor(out=self.f("bo2col"), in0=self.PS[6][:, 0:8],
                                         in1=self.f("bocol"), op=ALU.add), waits=[wT])

    def ln_boundary(self, x_src, write_out):
        g = self.g
        XB = lambda pt, tc: self.HLN[:, (tc % 2) * 8192 + pt * 512:(tc % 2) * 8192 + (pt + 1) * 512]
        SQ = lambda i: self.HLN[:, 4096 + (i % 2) * 512:4096 + (i % 2) * 512 + 512]
        def issue_loads(tc, extra=()):
            w = None
            for pt in range(NPT):
                w = self.sch.op("gpsimd", lambda e, pt=pt, tc=tc: e.dma_start(
                    out=XB(pt, tc), in_=x_src[pt, :, tc * TC:(tc + 1) * TC]),
                    tuple(extra), ((self.dVa[tc % 2], 16),))
            return w
        wall = {0: issue_loads(0, self.barrierTVS() + [self.now(self.dY)])}
        for tc in range(NTC):
            if tc + 1 < NTC:
                wall[tc + 1] = issue_loads(tc + 1, [self.now(self.sT), self.now(self.sS)])
            wmm7 = {}
            for pt in range(NPT):
                self.T(lambda e, pt=pt, tc=tc: e.matmul(
                    self.PS[6][0:1, 0:512], g("onescol"), XB(pt, tc),
                    start=(pt == 0), stop=(pt == NPT - 1)),
                    waits=[wall[tc]] + ([self.now(self.sV)] if pt == 0 else []))
                self.S(lambda e, pt=pt, tc=tc: e.activation(out=SQ(pt), in_=XB(pt, tc), func=AF.Square),
                       waits=[wall[tc]] + ([wmm7[pt - 2]] if pt >= 2 else []))
                wS = self.now(self.sS)
                self.T(lambda e, pt=pt: e.matmul(
                    self.PS[7][0:1, 0:512], g("onescol"), SQ(pt),
                    start=(pt == 0), stop=(pt == NPT - 1)), waits=[wS])
                wmm7[pt] = self.now(self.sT)
            wT = self.now(self.sT)
            wab = self.stats_to_ab(wT)
            TTf = self.hf(2560, 512)
            XF = lambda i: self.hf(3072 + (i % 2) * 512, 512)
            for pt in range(NPT):
                wxf = self.Xd(lambda e, pt=pt, tc=tc: e.dma_start(
                    out=XF(pt), in_=x_src[pt, :, tc * TC:(tc + 1) * TC]),
                    waits=[self.now(self.sV), self.now(self.dY)])
                self.V(lambda e, pt=pt, TTf=TTf: e.tensor_tensor(
                    out=TTf, in0=XF(pt), in1=self.PS[4][:, 0:512], op=ALU.mult),
                    waits=[wxf] + ([wab] if pt == 0 else []))
                write_out(pt, tc, TTf, self.PS[5][:, 0:512])

    def stats_to_ab(self, wT, gain=None, bias=None):
        g = self.g
        rr = self.R1
        self.V(lambda e: e.tensor_scalar(out=rr[0:1, 0:512], in0=self.PS[6][0:1, 0:512],
                                         scalar1=1.0 / 1024, scalar2=None, op0=ALU.mult),
               waits=[wT])
        self.V(lambda e: e.tensor_scalar(out=rr[0:1, 512:1024], in0=self.PS[7][0:1, 0:512],
                                         scalar1=1.0 / 1024, scalar2=None, op0=ALU.mult))
        self.V(lambda e: e.tensor_tensor(out=rr[0:1, 1024:1536], in0=rr[0:1, 0:512],
                                         in1=rr[0:1, 0:512], op=ALU.mult),
               waits=[self.now(self.sV)])
        self.V(lambda e: e.tensor_tensor(out=rr[0:1, 1536:2048], in0=rr[0:1, 512:1024],
                                         in1=rr[0:1, 1024:1536], op=ALU.subtract),
               waits=[self.now(self.sV)])
        wV = self.now(self.sV)
        self.S(lambda e: e.activation(out=rr[0:1, 2048:2560], in_=rr[0:1, 1536:2048],
                                      func=AF.Ln, bias=self.SC[0:1, 8:9], scale=1.0),
               waits=[wV, self.w_eps])
        self.S(lambda e: e.activation(out=rr[0:1, 2560:3072], in_=rr[0:1, 2048:2560],
                                      func=AF.Exp, scale=-0.5), waits=[self.now(self.sS)])
        wS = self.now(self.sS)
        self.V(lambda e: e.tensor_tensor(out=rr[0:1, 1024:1536], in0=rr[0:1, 0:512],
                                         in1=rr[0:1, 2560:3072], op=ALU.mult), waits=[wS])
        self.V(lambda e: e.tensor_scalar(out=rr[0:1, 512:1024], in0=rr[0:1, 1024:1536],
                                         scalar1=-1.0, scalar2=None, op0=ALU.mult),
               waits=[self.now(self.sV)])
        wV = self.now(self.sV)
        self.T(lambda e: e.matmul(self.PS[4][:, 0:512], self.f("ones1f", 0, 1),
                                  rr[0:1, 2560:3072], start=True, stop=True), waits=[wV])
        self.T(lambda e: e.matmul(self.PS[5][:, 0:512], self.f("ones1f", 0, 1),
                                  rr[0:1, 512:1024], start=True, stop=True))
        return self.now(self.sT)

    def body(self):
        self.ph0()
        if self.debug_stage == 54:
            wpre = [self.now(self.sS), self.now(self.sV), self.now(self.sT), self.now(self.dP)]
            self.Yd(lambda e: e.dma_start(out=self.dbg["d1"][0][0:1, 0:1536], in_=self.R1[0:1, 0:1536]), waits=wpre)
            self.Yd(lambda e: e.dma_start(out=self.dbg["d1"][0][1:2, 0:1536], in_=self.R1[0:1, 1536:3072]), waits=wpre)
            self.Yd(lambda e: e.dma_start(out=self.dbg["d1"][1][0:1, 0:64], in_=self.SC[:, :]), waits=wpre)
            self.Yd(lambda e: e.dma_start(out=self.dbg["d1"][2][:, 0:12], in_=self.f("signeg")), waits=wpre)
            self.Pd(lambda e: e.dma_start(out=self.dbg["d1"][3][0:1, 0:1024], in_=self.R1b[:, :]), waits=wpre)
            self.final_waits.append(self.now(self.dY))
            self.final_waits.append(self.now(self.dP))
            return
        nb = 1 if self.debug_stage else 2
        for b in range(nb):
            self.seq(b)

    def seq(self, b):
        self.zero_guard()
        def wz(pt, tc, t_ap, b_ap):
            self.V(lambda e, pt=pt, tc=tc: e.tensor_tensor(
                out=self.zsl(pt, tc * TC, TC), in0=t_ap, in1=b_ap, op=ALU.add),
                   waits=[self.now(self.sV)])
        self.ln_boundary(self.xin[b], wz)
        if self.debug_stage == 1:
            self.dump_z(); return
        if self.debug_stage in (21, 22, 23, 24):
            l = 0
            if self.debug_stage == 22:
                self.ffn_mlp(l)
                self.dump_z(); return
            if self.debug_stage == 24:
                self.mlp_hln_only = True
                self.ffn_mlp(l)
                wS = self.now(self.sS)
                for pt in range(NPT):
                    self.Pd(lambda e, pt=pt: e.dma_start(
                        out=self.dbg["d1"][pt], in_=self.HLN[:, pt * 2048:(pt + 1) * 2048]),
                        waits=[wS])
                self.final_waits.append(self.now(self.dP))
                return
            self.ffn_qkv(l)
            self.ffn_rotary(l)
            if self.debug_stage == 23:
                self.dump_qk(); return
            self.ffn_local_attn(l)
            self.dump_z(); return
        nlayers = 2
        for l in range(nlayers):
            self.ffn_layer(b, l)
            if self.debug_stage == 2 and l == 0:
                self.dump_z(); return
        if self.debug_stage == 3:
            self.dump_z(); return
        self.residual_x(self.xin[b], self.x_state[b])
        if self.debug_stage == 4:
            self.dump_dram(self.x_state[b]); return
        if self.debug_stage in (51, 52, 53, 55, 56, 57, 58):
            self.attn_dbg = self.debug_stage
        self.attn_sublayer(b)
        if self.debug_stage == 5:
            self.dump_dram(self.x_state[b]); return
        if self.debug_stage in (51, 52, 53, 55, 56, 57, 58):
            return
        self.zero_guard()
        def wz2(pt, tc, t_ap, b_ap):
            self.V(lambda e, pt=pt, tc=tc: e.tensor_tensor(
                out=self.zsl(pt, tc * TC, TC), in0=t_ap, in1=b_ap, op=ALU.add),
                   waits=[self.now(self.sV)])
        self.ln_boundary(self.x_state[b], wz2)
        for l in range(2):
            self.ffn_layer(b, l)
        self.residual_x(self.x_state[b], self.xout[b])

    def zero_guard(self):
        wpre = [self.now(self.sT), self.now(self.sS), self.now(self.dV)]
        for pt in range(NPT):
            self.V(lambda e, pt=pt: e.memset(self.A[:, pt * ZW:pt * ZW + 1], 0.0), waits=wpre)

    def dump_dram(self, src):
        w = self.final_waits[-1] if self.final_waits else self.now(self.dY)
        for pt in range(NPT):
            self.Yd(lambda e, pt=pt: e.dma_start(out=self.dbg["d1"][pt], in_=src[pt]), waits=[w])
        self.final_waits.append(self.now(self.dY))

    def dump_qk(self):
        wV = self.now(self.sV)
        for pt in range(NPT):
            self.Pd(lambda e, pt=pt: e.dma_start(
                out=self.dbg["d1"][pt], in_=self.C[:, pt * 2048:(pt + 1) * 2048]), waits=[wV])
        self.final_waits.append(self.now(self.dP))

    def dump_z(self):
        wV = self.now(self.sV)
        for pt in range(NPT):
            self.Yd(lambda e, pt=pt: e.dma_start(out=self.dbg["d1"][pt], in_=self.zsl(pt, 0, N)),
                    waits=[wV])
        self.final_waits.append(self.now(self.dY))

    def residual_x(self, x_src, x_dst):
        XC = lambda i: self.hf(4096 + (i % 4) * 512, 512)
        RC = lambda i: self.hf(6144 + (i % 4) * 512, 512)
        wpre = self.barrierTVS() + [self.now(self.dY)]
        vmark = {}
        smark = {}
        for pt in range(NPT):
            for tc in range(NTC):
                i = pt * NTC + tc
                w = self.sch.op("scalar", lambda e, pt=pt, tc=tc, i=i: e.dma_start(
                    out=XC(i), in_=x_src[pt, :, tc * TC:(tc + 1) * TC]),
                    tuple(wpre + ([vmark[i - 4]] if i >= 4 else [])),
                    ((self.dXa[i % 4], 16),))
                self.V(lambda e, pt=pt, tc=tc, i=i: e.scalar_tensor_tensor(
                    out=RC(i), in0=self.zsl(pt, tc * TC, TC), scalar=0.5,
                    in1=XC(i), op0=ALU.mult, op1=ALU.add),
                    waits=[w] + ([smark[i - 4]] if i >= 4 else []))
                vmark[i] = self.now(self.sV)
                smark[i] = self.sch.op("sync", lambda e, pt=pt, tc=tc, i=i: e.dma_start(
                    out=x_dst[pt, :, tc * TC:(tc + 1) * TC], in_=RC(i)),
                    (vmark[i],), ((self.dYa[i % 4], 16),))
        for j in range(4):
            self.final_waits.append(self.now(self.dYa[j]))

    def ffn_layer(self, b, l):
        self.ffn_qkv(l)
        self.ffn_rotary(l)
        self.ffn_local_attn(l)
        self.ffn_mlp(l)

    def conv_chunk(self, tc, waits=()):
        return self.sch.op("gpsimd", lambda e, tc=tc: e.dma_start(
            out=self.Fw[:, 0:4104],
            in_=self.A[:, :].rearrange("p (q c) -> p q c", q=8)[:, :, tc * TC:tc * TC + 513]),
            tuple(waits), ((self.dVc, 16),))

    def ffn_qkv(self, l):
        g = self.g
        WSL = lambda s: self.Fw[:, 4104 + s * 1024:4104 + s * 1024 + 1024]
        VW = self.HLN[:, 12288:12288 + 4096]
        self._vwctr = getattr(self, '_vwctr', 0) + 1
        wvw = self.wload("VW", f"ffnv{self._vwctr}", lambda e, l=l: e.dma_start(
            out=VW, in_=self.wqkv_vw[l]))
        conv = lambda pt: self.Fw[:, pt * 513:pt * 513 + 513]
        for tc in range(NTC):
            wcv = self.conv_chunk(tc, waits=[self.now(self.sT), self.now(self.sV)])
            for oc in range(8):
                wld = self.wload(("FW", oc % 2), "qkv", lambda e, oc=oc, l=l: e.dma_start(
                    out=WSL(oc % 2), in_=self.wqkv_slab[l, oc]))
                for kt in range(8):
                    sh = 1 if kt >= 4 else 0
                    self.T(lambda e, oc=oc, kt=kt, sh=sh: e.matmul(
                        self.PS[oc % 2][:, 0:512],
                        WSL(oc % 2)[:, kt * 128:kt * 128 + 128],
                        conv(kt)[:, 1 - sh:513 - sh],
                        start=(kt == 0), stop=(kt == 7)),
                        waits=[wld, wcv] + ([self.now(self.sS)] if kt == 0 else []))
                wT = self.now(self.sT)
                self.war[("FW", oc % 2)] = wT
                self.S(lambda e, oc=oc, tc=tc: e.copy(
                    self.C[:, oc * 2048 + tc * 512:oc * 2048 + tc * 512 + 512],
                    self.PS[oc % 2][:, 0:512]), waits=[wT])
            for ktok in range(4):
                kt_g = tc * 4 + ktok
                for kt in range(8):
                    sh = 1 if kt >= 4 else 0
                    self.T(lambda e, kt=kt, ktok=ktok, sh=sh: e.matmul(
                        self.PS[2 + (ktok % 2)][:, 0:512],
                        conv(kt)[:, 1 - sh + ktok * 128:1 - sh + ktok * 128 + 128],
                        VW[:, kt * 512:kt * 512 + 512],
                        start=(kt == 0), stop=(kt == 7)),
                        waits=[wcv, wvw] + ([self.now(self.sS)] if kt == 0 else []))
                wT = self.now(self.sT)
                self.war["VW"] = wT
                self.S(lambda e, kt_g=kt_g, ktok=ktok: e.copy(
                    self.B[:, kt_g * 528:kt_g * 528 + 528]
                        .rearrange("p (h c) -> p h c", h=8)[:, :, 0:64],
                    self.PS[2 + (ktok % 2)][:, 0:512]), waits=[wT])
        wS = self.now(self.sS)
        self.V(lambda e: e.memset(
            self.B[:, 0:16 * 528].rearrange("p (x c) -> p x c", c=66)[:, :, 64:65], 1.0),
            waits=[wS])

    def ffn_rotary(self, l):
        g = self.g
        tables = (self.c_cs, self.c_ss, self.c_ci, self.c_si)
        order = [("qscol", 0, 0), ("qsrot", 1, 1), ("kscol", 2, 2), ("ksrot", 3, 3)]
        for j, (cn, ti, _) in enumerate(order):
            wtb = self.Pd(lambda e, ti=ti: e.dma_start(
                out=self.HLN[:, 8192 + (ti % 2) * 2048:8192 + (ti % 2) * 2048 + 2048],
                in_=tables[ti]),
                waits=[self.now(self.sT), self.now(self.sS), self.now(self.sV)])
            self.S(lambda e, j=j, cn=cn, ti=ti, l=l: e.activation(
                out=self.HLN[:, j * 2048:(j + 1) * 2048],
                in_=self.HLN[:, 8192 + (ti % 2) * 2048:8192 + (ti % 2) * 2048 + 2048],
                func=AF.Identity, scale=self.f(cn, 0, 128, l, 1)),
                waits=[wtb])
        wprep = self.now(self.sS)
        scr1 = self.HLN[:, 8192:10240]
        scr2 = self.HLN[:, 10240:10752]
        scr3 = self.HLN[:, 10752:11264]
        for qk in range(2):
            coff = 0 if qk == 0 else 4096
            for pti in range(4):
                src = self.C[:, (qk * 4 + pti) * 2048:(qk * 4 + pti) * 2048 + 2048]
                self.S(lambda e, src=src: e.activation(out=scr1, in_=src, func=AF.Square),
                       waits=[self.now(self.sV), self.now(self.sT)])
                wS = self.now(self.sS)
                for tc in range(NTC):
                    self.T(lambda e, tc=tc: e.matmul(
                        self.PS[6 + (tc % 2)][0:2, 0:512], g("ones2"),
                        scr1[:, tc * 512:tc * 512 + 512], start=True, stop=True),
                        waits=[wS, self.now(self.sV), self.now(self.sS)])
                    wT = self.now(self.sT)
                    self.S(lambda e, tc=tc: e.activation(
                        out=self.R2[0:2, 0:512], in_=self.PS[6 + (tc % 2)][0:2, 0:512],
                        func=AF.Ln, bias=self.EPS2[0:2, 0:1], scale=1.0),
                        waits=[wT, self.w_eps])
                    self.S(lambda e: e.activation(out=self.R2[0:2, 512:1024],
                                                  in_=self.R2[0:2, 0:512], func=AF.Exp, scale=-0.5),
                           waits=[self.now(self.sS)])
                    wS2 = self.now(self.sS)
                    self.V(lambda e: e.tensor_copy(self.R2b[0:2, 0:512], self.R2[0:2, 512:1024]),
                           waits=[wS2])
                    wV = self.now(self.sV)
                    self.T(lambda e: e.matmul(self.PS[4][:, 0:512], g("pat2", 0, 2),
                                              self.R2b[0:2, 0:512], start=True, stop=True),
                           waits=[wV])
                    self.T(lambda e, src=src, tc=tc: e.matmul(
                        self.PS[5][:, 0:512], g("rot"), src[:, tc * 512:tc * 512 + 512],
                        start=True, stop=True), waits=[self.now(self.sV)])
                    wT2 = self.now(self.sT)
                    self.V(lambda e, src=src, tc=tc, coff=coff: e.tensor_tensor(
                        out=scr2, in0=src[:, tc * 512:tc * 512 + 512],
                        in1=self.HLN[:, coff + tc * 512:coff + tc * 512 + 512], op=ALU.mult),
                        waits=[wprep, wT2])
                    self.V(lambda e, tc=tc, coff=coff: e.tensor_tensor(
                        out=scr3, in0=self.PS[5][:, 0:512],
                        in1=self.HLN[:, coff + 2048 + tc * 512:coff + 2048 + tc * 512 + 512],
                        op=ALU.mult))
                    self.V(lambda e: e.tensor_tensor(out=scr2, in0=scr2, in1=scr3, op=ALU.add),
                           waits=[self.now(self.sV)])
                    self.V(lambda e, src=src, tc=tc: e.tensor_tensor(
                        out=src[:, tc * 512:tc * 512 + 512], in0=scr2,
                        in1=self.PS[4][:, 0:512], op=ALU.mult),
                           waits=[self.now(self.sV)])

    def ffn_local_attn(self, l):
        g = self.g
        OPT = 12416
        RB = self.HLN[0:64, 8192:8704]
        for qc in range(NTC):
            for h in range(LH):
                base = 64 * (h % 2)
                pti = h // 2
                tp = (64, 0) if base else None
                qsl = self.C[base:base + 64, pti * 2048 + qc * 512:pti * 2048 + qc * 512 + 512]
                kts = list(range(max(0, 4 * qc - 4), 4 * qc + 4))
                kts.sort(key=lambda kt: (0 if kt == 4 * qc else 1, kt))
                wexp_prev = {}
                exp_marks = {}
                spans = {}
                def emit_scores(i):
                    kt = kts[i]
                    ktr = kt - 4 * qc
                    j0 = max(0, ktr); j1 = min(3, ktr + 4)
                    c0 = 128 * j0; c1 = 128 * (j1 + 1)
                    spans[i] = (c0, c1)
                    pb = i % 2
                    ksl = self.C[base:base + 64,
                                 (4 + pti) * 2048 + kt * 128:(4 + pti) * 2048 + kt * 128 + 128]
                    mask_js = []
                    if ktr <= -1 and ktr + 4 <= 3:
                        mask_js.append(("mask4", ktr + 4))
                    if ktr >= 0:
                        mask_js.append(("mask0", ktr))
                    wW = [wexp_prev[pb]] if pb in wexp_prev else [self.now(self.sS)]
                    self.T(lambda e, ksl=ksl, qsl=qsl, pb=pb, c0=c0, c1=c1, tp=tp,
                           st=(len(mask_js) == 0): e.matmul(
                        self.PS[pb][:, c0:c1], ksl, qsl[:, c0:c1],
                        start=True, stop=st, tile_position=tp), waits=wW)
                    for mi, (mn, jm) in enumerate(mask_js):
                        self.T(lambda e, pb=pb, mn=mn, jm=jm, st=(mi == len(mask_js) - 1): e.matmul(
                            self.PS[pb][:, jm * 128:jm * 128 + 128], g(mn), g("ident"),
                            start=False, stop=st))
                def emit_exp(i):
                    pb = i % 2
                    c0, c1 = spans[i]
                    wT = self.now(self.sT)
                    self.S(lambda e, pb=pb, c0=c0, c1=c1, l=l: e.activation(
                        out=self.EXPB[:, pb * 1024 + c0:pb * 1024 + c1],
                        in_=self.PS[pb][:, c0:c1], func=AF.Exp,
                        bias=self.f("signeg", 0, 128, l, 1), scale=QK_SCALE), waits=[wT])
                    wexp_prev[pb] = self.now(self.sS)
                    exp_marks[i] = wexp_prev[pb]
                def emit_pv(i):
                    kt = kts[i]
                    pb = i % 2
                    c0, c1 = spans[i]
                    self.T(lambda e, kt=kt, h=h, pb=pb, c0=c0, c1=c1,
                           fp=(i == 0), lp=(i == len(kts) - 1): e.matmul(
                        self.PS[2][0:65, c0:c1],
                        self.B[:, kt * 528 + h * 66:kt * 528 + h * 66 + 65],
                        self.EXPB[:, pb * 1024 + c0:pb * 1024 + c1],
                        start=fp, stop=lp),
                        waits=[exp_marks[i]] + ([self.now(self.sV)] if i == 0 else []))
                for i in range(len(kts)):
                    emit_scores(i)
                    if i >= 2:
                        emit_pv(i - 2)
                    emit_exp(i)
                if len(kts) >= 2:
                    emit_pv(len(kts) - 2)
                emit_pv(len(kts) - 1)
                wTpv = self.now(self.sT)
                self.V(lambda e: e.tensor_scalar(
                    out=self.R2[0:1, 1024:1536], in0=self.PS[2][64:65, 0:512],
                    scalar1=1e-30, scalar2=None, op0=ALU.add), waits=[wTpv])
                wVd_ = self.now(self.sV)
                self.S(lambda e: e.activation(out=self.R2[0:1, 0:512], in_=self.R2[0:1, 1024:1536],
                                              func=AF.Ln), waits=[wVd_])
                self.S(lambda e: e.activation(out=self.R2b[0:1, 0:512], in_=self.R2[0:1, 0:512],
                                              func=AF.Exp, scale=-1.0), waits=[self.now(self.sS)])
                wV = self.now(self.sS)
                self.T(lambda e: e.matmul(self.PS[4][0:64, 0:512], g("pat2", 0, 2)[0:1, 0:64],
                                          self.R2b[0:1, 0:512], start=True, stop=True),
                       waits=[wV, self.now(self.sS)])
                wT2 = self.now(self.sT)
                self.S(lambda e: e.copy(RB, self.PS[4][0:64, 0:512]), waits=[wT2])
                wS3 = self.now(self.sS)
                if h % 2 == 0:
                    self.V(lambda e, pti=pti: e.tensor_tensor(
                        out=self.B[0:64, OPT + pti * 512:OPT + pti * 512 + 512],
                        in0=self.PS[2][0:64, 0:512], in1=RB, op=ALU.mult),
                        waits=[wS3, self.now(self.sT)])
                else:
                    self.V(lambda e: e.tensor_tensor(
                        out=self.ODD[0:64, 0:512], in0=self.PS[2][0:64, 0:512],
                        in1=RB, op=ALU.mult), waits=[wS3, self.now(self.dPo), self.now(self.sT)])
                    wV2 = self.now(self.sV)
                    self.sch.op("gpsimd", lambda e, pti=pti: e.dma_start(
                        out=self.B[64:128, OPT + pti * 512:OPT + pti * 512 + 512],
                        in_=self.ODD[0:64, 0:512]), (wV2,), ((self.dPo, 16),))
            wodd = self.now(self.dPo)
            wvo = self.now(self.sV)
            for oc in range(8):
                wld = self.wload(("FW", oc % 2), "wo", lambda e, oc=oc, l=l: e.dma_start(
                    out=self.Fw[:, 4104 + (oc % 2) * 1024:4104 + (oc % 2) * 1024 + 512],
                    in_=self.wo_slab[l, oc]))
                for kt in range(4):
                    self.T(lambda e, oc=oc, kt=kt: e.matmul(
                        self.PS[5 + (oc % 2)][:, 0:512],
                        self.Fw[:, 4104 + (oc % 2) * 1024 + kt * 128:4104 + (oc % 2) * 1024 + kt * 128 + 128],
                        self.B[:, OPT + kt * 512:OPT + kt * 512 + 512],
                        start=(kt == 0), stop=(kt == 3)),
                        waits=[wld, wodd, wvo] + ([self.now(self.sV)] if kt == 0 else []))
                wT = self.now(self.sT)
                self.war[("FW", oc % 2)] = wT
                self.V(lambda e, oc=oc, qc=qc: e.tensor_tensor(
                    out=self.zsl(oc, qc * 512, 512), in0=self.zsl(oc, qc * 512, 512),
                    in1=self.PS[5 + (oc % 2)][:, 0:512], op=ALU.add), waits=[wT])

    def ffn_mlp(self, l):
        g = self.g
        conv = lambda pt: self.Fw[:, pt * 513:pt * 513 + 513]
        SQ = lambda i: self.B[:, 12416 + (i % 2) * 512:12416 + (i % 2) * 512 + 512]
        for tc in range(NTC):
            wcv = self.conv_chunk(tc, waits=[self.now(self.sT), self.now(self.sS), self.now(self.sV)])
            wmm7 = {}
            for pt in range(NPT):
                sh = 1 if pt >= 4 else 0
                self.T(lambda e, pt=pt, sh=sh: e.matmul(
                    self.PS[6][0:1, 0:512], g("onescol"), conv(pt)[:, 1 - sh:513 - sh],
                    start=(pt == 0), stop=(pt == NPT - 1)), waits=[wcv])
                self.S(lambda e, pt=pt, sh=sh: e.activation(
                    out=SQ(pt), in_=conv(pt)[:, 1 - sh:513 - sh], func=AF.Square),
                    waits=[wcv] + ([wmm7[pt - 2]] if pt >= 2 else []))
                wS = self.now(self.sS)
                self.T(lambda e, pt=pt: e.matmul(
                    self.PS[7][0:1, 0:512], g("onescol"), SQ(pt),
                    start=(pt == 0), stop=(pt == NPT - 1)), waits=[wS])
                wmm7[pt] = self.now(self.sT)
            wT = self.now(self.sT)
            wab = self.stats_to_ab(wT)
            wS_prev = {}
            for pt in range(NPT):
                sh = 1 if pt >= 4 else 0
                TTf = self.B[:, 13440 + (pt % 2) * 1024:14464 + (pt % 2) * 1024].bitcast(F32)
                zin = self.A[:, pt * ZW + tc * TC + 1 - sh:pt * ZW + tc * TC + 1 - sh + 512]
                self.V(lambda e, pt=pt, zin=zin, TTf=TTf: e.tensor_tensor(
                    out=TTf, in0=zin, in1=self.PS[4][:, 0:512], op=ALU.mult),
                    waits=([wab] if pt == 0 else []) + ([wS_prev[pt - 2]] if pt >= 2 else []))
                self.V(lambda e, TTf=TTf: e.tensor_tensor(
                    out=TTf, in0=TTf, in1=self.PS[5][:, 0:512], op=ALU.add),
                       waits=[self.now(self.sV)])
                wV = self.now(self.sV)
                self.S(lambda e, pt=pt, tc=tc, l=l, TTf=TTf: e.activation(
                    out=self.HLN[:, pt * 2048 + tc * 512:pt * 2048 + tc * 512 + 512],
                    in_=TTf, func=AF.Identity,
                    scale=self.f("lngcol", 0, 128, l * 8 + pt, 1),
                    bias=self.f("lnbcol", 0, 128, l * 8 + pt, 1)), waits=[wV])
                wS_prev[pt] = self.now(self.sS)
        if getattr(self, 'mlp_hln_only', False):
            return
        W1S = lambda s: self.Fw[:, 4104 + s * 1024:4104 + s * 1024 + 1024]
        nslab = 22
        for tc in range(NTC):
            whl = self.now(self.sS)
            for sl in range(nslab):
                wlda = self.wload(("FW", 0), "w1a", lambda e, sl=sl, l=l: e.dma_start(
                    out=W1S(0), in_=self.w1a_slab[l, sl]))
                wldg = self.wload(("FW", 1), "w1g", lambda e, sl=sl, l=l: e.dma_start(
                    out=W1S(1), in_=self.w1g_slab[l, sl]))
                for kt in range(8):
                    self.T(lambda e, sl=sl, kt=kt, tc=tc: e.matmul(
                        self.PS[0][:, 0:512], W1S(0)[:, kt * 128:kt * 128 + 128],
                        self.HLN[:, kt * 2048 + tc * 512:kt * 2048 + tc * 512 + 512],
                        start=(kt == 0), stop=(kt == 7)),
                        waits=[wlda, whl] + ([self.now(self.sS), self.now(self.sV)] if kt == 0 else []))
                self.war[("FW", 0)] = self.now(self.sT)
                for kt in range(8):
                    self.T(lambda e, sl=sl, kt=kt, tc=tc: e.matmul(
                        self.PS[1][:, 0:512], W1S(1)[:, kt * 128:kt * 128 + 128],
                        self.HLN[:, kt * 2048 + tc * 512:kt * 2048 + tc * 512 + 512],
                        start=(kt == 0), stop=(kt == 7)),
                        waits=[wldg] + ([self.now(self.sS), self.now(self.sV)] if kt == 0 else []))
                wT = self.now(self.sT)
                self.war[("FW", 1)] = wT
                self.S(lambda e, sl=sl, l=l: e.activation(
                    out=self.EXPB[:, 0:512], in_=self.PS[0][:, 0:512], func=AF.Identity,
                    bias=self.f("b1acol", 0, 128, l * 22 + sl, 1)), waits=[wT])
                self.S(lambda e, sl=sl, l=l: e.activation(
                    out=self.EXPB[:, 512:1024], in_=self.PS[1][:, 0:512], func=AF.Gelu,
                    bias=self.f("b1gcol", 0, 128, l * 22 + sl, 1)))
                wS = self.now(self.sS)
                self.V(lambda e, sl=sl: e.tensor_tensor(
                    out=self.B[:, sl * 512:sl * 512 + 512],
                    in0=self.EXPB[:, 0:512], in1=self.EXPB[:, 512:1024],
                    op=ALU.mult), waits=[wS])
            wh1 = self.now(self.sV)
            for oc in range(8):
                wldk = self.wload(("FWC", oc % 2), "w2", lambda e, oc=oc, l=l: e.dma_start(
                    out=self.Fw[:, (oc % 2) * 2816:(oc % 2) * 2816 + 2816],
                    in_=self.w2t[l, oc]),
                    extra=[self.war.get(("FW", 0)), self.war.get(("FW", 1))])
                for kt in range(22):
                    self.T(lambda e, oc=oc, kt=kt: e.matmul(
                        self.PS[2 + (oc % 2)][:, 0:512],
                        self.Fw[:, (oc % 2) * 2816 + kt * 128:(oc % 2) * 2816 + kt * 128 + 128],
                        self.B[:, kt * 512:kt * 512 + 512],
                        start=(kt == 0), stop=(kt == 21)),
                        waits=[wldk, wh1] + ([self.now(self.sV)] if kt == 0 else []))
                wT = self.now(self.sT)
                self.war[("FWC", oc % 2)] = wT
                self.war[("FW", 0)] = wT
                self.war[("FW", 1)] = wT
                self.V(lambda e, oc=oc, tc=tc, l=l: e.scalar_tensor_tensor(
                    out=self.zsl(oc, tc * 512, 512), in0=self.PS[2 + (oc % 2)][:, 0:512],
                    scalar=self.f("b2col", 0, 128, l * 8 + oc, 1),
                    in1=self.zsl(oc, tc * 512, 512), op0=ALU.add, op1=ALU.add), waits=[wT])

    def attn_sublayer(self, b):
        g = self.g; f = self.f
        Abf = self.A[:, :].bitcast(BF16)
        QT = lambda pt: Abf[:, pt * 2048:(pt + 1) * 2048]
        KT = lambda pt: Abf[:, 16384 + pt * 2048:16384 + (pt + 1) * 2048]
        YC = lambda pt: self.Fw[:, pt * 512:pt * 512 + 512]
        WSL = lambda s: self.Fw[:, 4104 + s * 1024:4104 + s * 1024 + 1024]
        XB = lambda pt, tc: (self.HLN[:, pt * 512:(pt + 1) * 512] if tc % 2 == 0
                             else self.C[:, pt * 512:(pt + 1) * 512])
        SQ = lambda i: self.HLN[:, 4096 + (i % 2) * 512:4096 + (i % 2) * 512 + 512]
        WVT = self.HLN[:, 8192:16384]
        self._vwctr = getattr(self, '_vwctr', 0) + 1
        wvw = self.wload("WVT", f"attnv{self._vwctr}", lambda e: e.dma_start(
            out=WVT, in_=self.WvTr))
        def issue_loads(tc, extra=()):
            w = None
            for pt in range(NPT):
                w = self.sch.op("gpsimd", lambda e, pt=pt, tc=tc, b=b: e.dma_start(
                    out=XB(pt, tc), in_=self.x_state[b][pt, :, tc * TC:(tc + 1) * TC]),
                    tuple(extra), ((self.dVa[tc % 2], 16),))
            return w
        wall = {0: issue_loads(0, self.barrierTVS() + [self.now(self.dY)])}
        for tc in range(NTC):
            if tc + 1 < NTC:
                wall[tc + 1] = issue_loads(tc + 1, [self.now(self.sT), self.now(self.sS)])
            wmm7 = {}
            for pt in range(NPT):
                self.T(lambda e, pt=pt, tc=tc: e.matmul(
                    self.PS[6][0:1, 0:512], g("onescol"), XB(pt, tc),
                    start=(pt == 0), stop=(pt == NPT - 1)),
                    waits=[wall[tc]] + ([self.now(self.sV)] if pt == 0 else []))
                self.S(lambda e, pt=pt, tc=tc: e.activation(out=SQ(pt), in_=XB(pt, tc), func=AF.Square),
                       waits=[wall[tc]] + ([wmm7[pt - 2]] if pt >= 2 else []))
                wS = self.now(self.sS)
                self.T(lambda e, pt=pt: e.matmul(
                    self.PS[7][0:1, 0:512], g("onescol"), SQ(pt),
                    start=(pt == 0), stop=(pt == NPT - 1)), waits=[wS])
                wmm7[pt] = self.now(self.sT)
            wT = self.now(self.sT)
            wab = self.stats_to_ab(wT)
            TTf = self.hf(2560, 512)
            XF = lambda i: self.hf(3072 + (i % 2) * 512, 512)
            for pt in range(NPT):
                wxf = self.Xd(lambda e, pt=pt, tc=tc, b=b: e.dma_start(
                    out=XF(pt), in_=self.x_state[b][pt, :, tc * TC:(tc + 1) * TC]),
                    waits=[self.now(self.sV), self.now(self.dY)])
                self.V(lambda e, pt=pt, TTf=TTf: e.tensor_tensor(
                    out=TTf, in0=XF(pt), in1=self.PS[4][:, 0:512], op=ALU.mult),
                    waits=[wxf] + ([wab, self.now(self.sT)] if pt == 0 else []))
                self.V(lambda e, pt=pt, TTf=TTf: e.tensor_tensor(
                    out=YC(pt), in0=TTf, in1=self.PS[5][:, 0:512], op=ALU.add),
                       waits=[self.now(self.sV)])
            wy = self.now(self.sV)
            for which, Wnm, bcol, dst in ((0, "Wq", "bqcol", QT),
                                          (1, "Wk", "bkcol", KT)):
                for oc in range(8):
                    wld = self.wload(("FW", oc % 2), "attnqk", lambda e, oc=oc, Wnm=Wnm: e.dma_start(
                        out=WSL(oc % 2), in_=self.Wslab[Wnm][oc]))
                    for kt in range(8):
                        self.T(lambda e, oc=oc, kt=kt: e.matmul(
                            self.PS[oc % 2][:, 0:512],
                            WSL(oc % 2)[:, kt * 128:kt * 128 + 128], YC(kt),
                            start=(kt == 0), stop=(kt == 7)),
                            waits=[wld, wy] + ([self.now(self.sS)] if kt == 0 else []))
                    wTq = self.now(self.sT)
                    self.war[("FW", oc % 2)] = wTq
                    self.S(lambda e, oc=oc, tc=tc, bcol=bcol, dst=dst: e.activation(
                        out=dst(oc)[:, tc * 512:tc * 512 + 512],
                        in_=self.PS[oc % 2][:, 0:512], func=AF.Identity,
                        bias=self.f(bcol, 0, 128, oc, 1)), waits=[wTq])
            for ktok in range(4):
                kt_g = tc * 4 + ktok
                for half in (1, 0):
                    for kt in range(8):
                        self.T(lambda e, kt=kt, ktok=ktok, half=half: e.matmul(
                            self.PS[3 - half][:, 0:512],
                            YC(kt)[:, ktok * 128:ktok * 128 + 128],
                            WVT[:, kt * 1024 + half * 512:kt * 1024 + half * 512 + 512],
                            start=(kt == 0), stop=(kt == 7)),
                            waits=[wvw, wy] + ([self.now(self.sS), self.now(self.sV)] if kt == 0 else []))

                wTv = self.now(self.sT)
                if getattr(self, 'attn_dbg', 0) == 57 and tc == 0 and ktok == 0:
                    self.V(lambda e: e.tensor_copy(self.hf(2560, 512), self.PS[3][:, 0:512]), waits=[wTv])
                    self.V(lambda e: e.tensor_copy(self.hf(4608, 512), self.PS[2][:, 0:512]))
                    wVd = self.now(self.sV)
                    self.Yd(lambda e: e.dma_start(out=self.dbg["d1"][6][:, 0:512], in_=self.hf(2560, 512)), waits=[wVd])
                    self.Yd(lambda e: e.dma_start(out=self.dbg["d1"][7][:, 0:512], in_=self.hf(4608, 512)), waits=[wVd])
                    self.final_waits.append(self.now(self.dY))
                for half in range(2):
                    self.S(lambda e, kt_g=kt_g, half=half: e.copy(
                        self.B[:, kt_g * 1056:kt_g * 1056 + 1056]
                            .rearrange("p (h c) -> p h c", h=16)[:, half * 8:half * 8 + 8, 0:64],
                        self.PS[3 - half][:, 0:512]), waits=[wTv])
                if getattr(self, 'attn_dbg', 0) == 58 and tc == 0 and ktok == 0:
                    w58 = [self.now(self.sS)]
                    self.Pd(lambda e: e.dma_start(out=self.dbg["d1"][6][:, 0:1056], in_=self.B[:, 0:1056]), waits=w58)
                    self.Pd(lambda e: e.dma_start(out=self.dbg["d1"][5][:, 0:2048], in_=WVT[:, 0:2048]), waits=w58)
                    self.final_waits.append(self.now(self.dP))
            for h in range(NH):
                base = 64 * (h % 2)
                pti = h // 2
                tp = (64, 0) if base else None
                pg = self.PS[5] if h % 2 == 0 else self.PS[4]
                for qt in range(4):
                    self.T(lambda e, h=h, qt=qt, base=base, pti=pti, tp=tp, pg=pg: e.matmul(
                        pg[:, 2 * qt:2 * qt + 2],
                        YC(pti)[base:base + 64, qt * 128:qt * 128 + 128],
                        self.g("gw2", base, base + 64),
                        start=True, stop=True, tile_position=tp),
                        waits=[wy, self.w_gw2] +
                              ([self.now(self.sV), self.now(self.sS)] if qt == 0 else []))
                wTg = self.now(self.sT)
                self.V(lambda e, pg=pg: e.tensor_tensor(
                    out=self.hf(2560, 8), in0=pg[:, 0:8], in1=self.g("gbb", 0, 128, 0, 8),
                    op=ALU.add), waits=[wTg, self.now(self.sS)])
                wV = self.now(self.sV)
                self.S(lambda e: e.activation(out=self.hf(2568, 8), in_=self.hf(2560, 8),
                                              func=AF.Sigmoid), waits=[wV])
                wSg = self.now(self.sS)
                sg = self.hf(2568, 8).rearrange("p (a c) -> p a c", c=2)
                self.V(lambda e, h=h, sg=sg: e.tensor_scalar(
                    out=self.hf(2576, 4), in0=sg[:, :, 1],
                    scalar1=self.f("gruccol", 0, 128, h, 1), scalar2=-1.0,
                    op0=ALU.mult, op1=ALU.add), waits=[wSg])
                self.V(lambda e, sg=sg: e.tensor_tensor(
                    out=self.hf(2580, 4), in0=self.hf(2576, 4), in1=sg[:, :, 0], op=ALU.mult),
                    waits=[self.now(self.sV)])
                self.V(lambda e, h=h, tc=tc: e.tensor_scalar(
                    out=self.f("gates", 0, 128, h * 16 + tc * 4, 4), in0=self.hf(2580, 4),
                    scalar1=2.0, scalar2=None, op0=ALU.add), waits=[self.now(self.sV)])
        wS = self.now(self.sS)
        self.V(lambda e: e.memset(
            self.B[:, 0:16 * 1056].rearrange("p (x c) -> p x c", c=66)[:, :, 64:65],
            1.0), waits=[wS])
        wvones = self.now(self.sV)
        wgates = wvones
        if getattr(self, 'attn_dbg', 0) == 51:
            for pt in range(NPT):
                self.Pd(lambda e, pt=pt: e.dma_start(
                    out=self.dbg["d1"][pt], in_=QT(pt)), waits=[wvones])
            self.final_waits.append(self.now(self.dP))
            return
        if getattr(self, 'attn_dbg', 0) == 52:
            for pt in range(NPT):
                self.Pd(lambda e, pt=pt: e.dma_start(
                    out=self.dbg["d1"][pt], in_=KT(pt)), waits=[wvones])
            self.final_waits.append(self.now(self.dP))
            return
        if getattr(self, 'attn_dbg', 0) == 53:
            self.attn_dump53 = True

        for pti in range(8):
            i0 = 1024 + pti * 16
            for which, srcf in ((0, QT), (1, KT)):
                self.S(lambda e, srcf=srcf, pti=pti: e.activation(
                    out=self.HLN[:, 8192:10240], in_=srcf(pti), func=AF.Square),
                    waits=[self.now(self.sV), self.now(self.sT)])
                wS = self.now(self.sS)
                for tc in range(NTC):
                    self.T(lambda e, tc=tc: e.matmul(
                        self.PS[7][0:1, 0:512], g("onescol"),
                        self.HLN[:, 8192 + tc * 512:8192 + tc * 512 + 512],
                        start=True, stop=True),
                        waits=([wS] if tc == 0 else []) + [self.now(self.sV)])
                    wT = self.now(self.sT)
                    self.V(lambda e, tc=tc, which=which, i0=i0: e.tensor_reduce(
                        self.R1[0:1, i0 + which * 8 + tc:i0 + which * 8 + tc + 1],
                        self.PS[7][0:1, 0:512], axis=AX.X, op=ALU.max), waits=[wT])
                self.V(lambda e, which=which, i0=i0: e.tensor_reduce(
                    self.R1[0:1, i0 + which * 8 + 4:i0 + which * 8 + 5],
                    self.R1[0:1, i0 + which * 8:i0 + which * 8 + 4],
                    axis=AX.X, op=ALU.max), waits=[self.now(self.sV)])
            wV = self.now(self.sV)
            self.S(lambda e, i0=i0: e.sqrt(self.R1[0:1, i0 + 5:i0 + 6], self.R1[0:1, i0 + 4:i0 + 5]),
                   waits=[wV])
            self.S(lambda e, i0=i0: e.sqrt(self.R1[0:1, i0 + 13:i0 + 14], self.R1[0:1, i0 + 12:i0 + 13]))
            wS = self.now(self.sS)
            self.V(lambda e, i0=i0: e.tensor_tensor(
                out=self.R1[0:1, i0 + 6:i0 + 7], in0=self.R1[0:1, i0 + 5:i0 + 6],
                in1=self.R1[0:1, i0 + 13:i0 + 14], op=ALU.mult), waits=[wS])
            self.V(lambda e, i0=i0: e.tensor_scalar(
                out=self.R1[0:1, i0 + 7:i0 + 8], in0=self.R1[0:1, i0 + 6:i0 + 7],
                scalar1=0.125, scalar2=None, op0=ALU.mult), waits=[self.now(self.sV)])
            self.V(lambda e, i0=i0: e.tensor_tensor(
                out=self.R1[0:1, i0 + 14:i0 + 15], in0=self.R1[0:1, i0 + 7:i0 + 8],
                in1=self.SC[0:1, 6:7], op=ALU.add),
                   waits=[self.w_bterm, self.now(self.sV)])
            self.V(lambda e, i0=i0, pti=pti: e.tensor_scalar(
                out=self.R1b[0:1, 50 + pti:51 + pti], in0=self.R1[0:1, i0 + 14:i0 + 15],
                scalar1=-1.0, scalar2=None, op0=ALU.mult), waits=[self.now(self.sV)])
            wV2 = self.now(self.sV)
            self.T(lambda e, pti=pti: e.matmul(self.PS[4][:, 20:21], self.g("ones1", 0, 1),
                                               self.R1b[0:1, 50 + pti:51 + pti],
                                               start=True, stop=True),
                   waits=[wV2, self.now(self.sS)])
            wT = self.now(self.sT)
            self.S(lambda e, pti=pti: e.copy(self.f("signeg", 0, 128, 2 + pti, 1),
                                             self.PS[4][:, 20:21]), waits=[wT])

        if getattr(self, 'attn_dbg', 0) == 56:
            wpre = [self.now(self.sS), self.now(self.sV), self.now(self.sT), self.now(self.dP)]
            self.Pd(lambda e: e.dma_start(out=self.dbg["d1"][6][:, 0:1056], in_=self.B[:, 0:1056]), waits=wpre)
            self.Pd(lambda e: e.dma_start(out=self.dbg["d1"][5][:, 0:2048], in_=WVT[:, 0:2048]), waits=wpre)
            self.final_waits.append(self.now(self.dP))
            return
        if getattr(self, 'attn_dump53', False):
            wpre = [self.now(self.sS), self.now(self.sV), self.now(self.sT)]
            self.Yd(lambda e: e.dma_start(
                out=self.dbg["d1"][0][:, 0:256], in_=self.f("gates")), waits=wpre)
            self.Yd(lambda e: e.dma_start(
                out=self.dbg["d1"][0][:, 256:268], in_=self.f("signeg")), waits=wpre)
            self.final_waits.append(self.now(self.dY))
            return
        GATED = lambda qt: self.C[:, 4096 + qt * 2048:4096 + qt * 2048 + 2048]
        STRIP = self.C[:, 12288:12288 + 3968]
        CTX = lambda pt: self.C[:, pt * 512:(pt + 1) * 512]
        RB = self.HLN[0:64, 8192:8704]
        for qc in range(NTC):
            for h in range(NH):
                base = 64 * (h % 2)
                pti = h // 2
                tp = (64, 0) if base else None
                widx = self.sch.op("gpsimd", lambda e, h=h: e.dma_start(
                    out=self.HLN[:, 10240:10242].bitcast(U32),
                    in_=self.c_stripidx[h]), (self.now(self.sT),), ((self.dPi, 16),))
                wstrip = self.sch.op("gpsimd", lambda e: e.indirect_dma_start(
                    out=STRIP, out_offset=None, in_=self.dv_dram,
                    in_offset=IndirectOffsetOnAxis(
                        ap=self.HLN[:, 10240:10242].bitcast(U32), axis=0)),
                    (widx, self.w_dv, self.now(self.sV), self.now(self.sT)),
                    ((self.dPs, 16),))
                wg = []
                for qt in range(4):
                    qtg = qc * 4 + qt
                    w = self.V(lambda e, qt=qt, qtg=qtg, h=h: e.tensor_scalar(
                        out=GATED(qt),
                        in0=STRIP[:, (15 - qtg) * 128:(15 - qtg) * 128 + 2048],
                        scalar1=self.f("gates", 0, 128, h * 16 + qtg, 1),
                        scalar2=QK_SCALE, op0=ALU.mult, op1=ALU.mult),
                        waits=[wstrip, wgates, self.now(self.sT)])
                    wg.append(w)
                sneg = self.f("signeg", 0, 128, 2 + pti, 1)
                wexp_prev = {}
                exp_marks = {}
                def emit_scores(pr):
                    prb = pr % 2
                    PT = self.PSW if prb == 0 else self.PSX
                    base_ps = 0 if prb == 0 else 6
                    wW = ([wexp_prev[prb]] if prb in wexp_prev else [self.now(self.sS)]) + wg
                    for half in range(2):
                        kt = 2 * pr + half
                        ksl = KT(pti)[base:base + 64, kt * 128:kt * 128 + 128]
                        qsl = QT(pti)[base:base + 64, qc * 512:qc * 512 + 512]
                        self.T(lambda e, ksl=ksl, qsl=qsl, base_ps=base_ps, half=half, tp=tp: e.matmul(
                            self.PS[base_ps + half][:, 0:512], ksl, qsl,
                            start=True, stop=False, tile_position=tp),
                            waits=wW if half == 0 else ())
                        for qt in range(4):
                            self.T(lambda e, base_ps=base_ps, half=half, qt=qt, kt=kt, st=(qt == 3): e.matmul(
                                self.PS[base_ps + half][:, qt * 128:qt * 128 + 128],
                                GATED(qt)[:, kt * 128:kt * 128 + 128], g("ident"),
                                start=False, stop=st))
                def emit_exp(pr):
                    prb = pr % 2
                    PT = self.PSW if prb == 0 else self.PSX
                    wT = self.now(self.sT)
                    self.S(lambda e, PT=PT, prb=prb, sneg=sneg: e.activation(
                        out=self.EXPB[:, prb * 1024:prb * 1024 + 1024],
                        in_=PT[:, 0:1024], func=AF.Exp,
                        bias=sneg, scale=0.125), waits=[wT])
                    wexp_prev[prb] = self.now(self.sS)
                    exp_marks[pr] = wexp_prev[prb]
                def emit_pv(pr):
                    prb = pr % 2
                    for half in range(2):
                        kt = 2 * pr + half
                        self.T(lambda e, kt=kt, h=h, prb=prb, half=half: e.matmul(
                            self.PS[2][0:65, 0:512],
                            self.B[:, kt * 1056 + h * 66:kt * 1056 + h * 66 + 65],
                            self.EXPB[:, prb * 1024 + half * 512:prb * 1024 + half * 512 + 512],
                            start=(kt == 0), stop=(kt == 15)),
                            waits=[exp_marks[pr], wvones] + ([self.now(self.sV)] if kt == 0 else []))
                for pr in range(8):
                    emit_scores(pr)
                    if pr >= 2:
                        emit_pv(pr - 2)
                    emit_exp(pr)
                emit_pv(6)
                emit_pv(7)
                wTpv = self.now(self.sT)
                self.V(lambda e: e.tensor_scalar(
                    out=self.R2[0:1, 1024:1536], in0=self.PS[2][64:65, 0:512],
                    scalar1=1e-30, scalar2=None, op0=ALU.add), waits=[wTpv])
                wVd_ = self.now(self.sV)
                self.S(lambda e: e.activation(out=self.R2[0:1, 0:512], in_=self.R2[0:1, 1024:1536],
                                              func=AF.Ln), waits=[wVd_])
                self.S(lambda e: e.activation(out=self.R2b[0:1, 0:512], in_=self.R2[0:1, 0:512],
                                              func=AF.Exp, scale=-1.0), waits=[self.now(self.sS)])
                wV = self.now(self.sS)
                self.T(lambda e: e.matmul(self.PS[4][0:64, 0:512], g("pat2", 0, 2)[0:1, 0:64],
                                          self.R2b[0:1, 0:512], start=True, stop=True),
                       waits=[wV, self.now(self.sS)])
                wT2 = self.now(self.sT)
                self.S(lambda e: e.copy(RB, self.PS[4][0:64, 0:512]), waits=[wT2])
                wS3 = self.now(self.sS)
                if h % 2 == 0:
                    self.V(lambda e, pti=pti: e.tensor_tensor(
                        out=self.B[0:64, OPT + pti * 512:OPT + pti * 512 + 512],
                        in0=self.PS[2][0:64, 0:512], in1=RB, op=ALU.mult),
                        waits=[wS3, self.now(self.sT)])
                else:
                    self.V(lambda e: e.tensor_tensor(
                        out=self.ODD[0:64, 0:512], in0=self.PS[2][0:64, 0:512],
                        in1=RB, op=ALU.mult), waits=[wS3, self.now(self.dPo), self.now(self.sT)])
                    wV2 = self.now(self.sV)
                    self.sch.op("gpsimd", lambda e, pti=pti: e.dma_start(
                        out=self.B[64:128, OPT + pti * 512:OPT + pti * 512 + 512],
                        in_=self.ODD[0:64, 0:512]), (wV2,), ((self.dPo, 16),))
            wodd = self.now(self.dPo)
            wvo = self.now(self.sV)
            for oc in range(8):
                wld = self.wload(("FW", oc % 2), "wo", lambda e, oc=oc, l=l: e.dma_start(
                    out=self.Fw[:, 4104 + (oc % 2) * 1024:4104 + (oc % 2) * 1024 + 512],
                    in_=self.wo_slab[l, oc]))
                for kt in range(4):
                    self.T(lambda e, oc=oc, kt=kt: e.matmul(
                        self.PS[5 + (oc % 2)][:, 0:512],
                        self.Fw[:, 4104 + (oc % 2) * 1024 + kt * 128:4104 + (oc % 2) * 1024 + kt * 128 + 128],
                        self.B[:, OPT + kt * 512:OPT + kt * 512 + 512],
                        start=(kt == 0), stop=(kt == 3)),
                        waits=[wld, wodd, wvo] + ([self.now(self.sV)] if kt == 0 else []))
                wT = self.now(self.sT)
                self.war[("FW", oc % 2)] = wT
                self.V(lambda e, oc=oc, qc=qc: e.tensor_tensor(
                    out=self.zsl(oc, qc * 512, 512), in0=self.zsl(oc, qc * 512, 512),
                    in1=self.PS[5 + (oc % 2)][:, 0:512], op=ALU.add), waits=[wT])

    def ffn_mlp(self, l):
        g = self.g
        conv = lambda pt: self.Fw[:, pt * 513:pt * 513 + 513]
        SQ = lambda i: self.B[:, 12416 + (i % 2) * 512:12416 + (i % 2) * 512 + 512]
        for tc in range(NTC):
            wcv = self.conv_chunk(tc, waits=[self.now(self.sT), self.now(self.sS), self.now(self.sV)])
            wmm7 = {}
            for pt in range(NPT):
                sh = 1 if pt >= 4 else 0
                self.T(lambda e, pt=pt, sh=sh: e.matmul(
                    self.PS[6][0:1, 0:512], g("onescol"), conv(pt)[:, 1 - sh:513 - sh],
                    start=(pt == 0), stop=(pt == NPT - 1)), waits=[wcv])
                self.S(lambda e, pt=pt, sh=sh: e.activation(
                    out=SQ(pt), in_=conv(pt)[:, 1 - sh:513 - sh], func=AF.Square),
                    waits=[wcv] + ([wmm7[pt - 2]] if pt >= 2 else []))
                wS = self.now(self.sS)
                self.T(lambda e, pt=pt: e.matmul(
                    self.PS[7][0:1, 0:512], g("onescol"), SQ(pt),
                    start=(pt == 0), stop=(pt == NPT - 1)), waits=[wS])
                wmm7[pt] = self.now(self.sT)
            wT = self.now(self.sT)
            wab = self.stats_to_ab(wT)
            wS_prev = {}
            for pt in range(NPT):
                sh = 1 if pt >= 4 else 0
                TTf = self.B[:, 13440 + (pt % 2) * 1024:14464 + (pt % 2) * 1024].bitcast(F32)
                zin = self.A[:, pt * ZW + tc * TC + 1 - sh:pt * ZW + tc * TC + 1 - sh + 512]
                self.V(lambda e, pt=pt, zin=zin, TTf=TTf: e.tensor_tensor(
                    out=TTf, in0=zin, in1=self.PS[4][:, 0:512], op=ALU.mult),
                    waits=([wab] if pt == 0 else []) + ([wS_prev[pt - 2]] if pt >= 2 else []))
                self.V(lambda e, TTf=TTf: e.tensor_tensor(
                    out=TTf, in0=TTf, in1=self.PS[5][:, 0:512], op=ALU.add),
                       waits=[self.now(self.sV)])
                wV = self.now(self.sV)
                self.S(lambda e, pt=pt, tc=tc, l=l, TTf=TTf: e.activation(
                    out=self.HLN[:, pt * 2048 + tc * 512:pt * 2048 + tc * 512 + 512],
                    in_=TTf, func=AF.Identity,
                    scale=self.f("lngcol", 0, 128, l * 8 + pt, 1),
                    bias=self.f("lnbcol", 0, 128, l * 8 + pt, 1)), waits=[wV])
                wS_prev[pt] = self.now(self.sS)
        if getattr(self, 'mlp_hln_only', False):
            return
        W1S = lambda s: self.Fw[:, 4104 + s * 1024:4104 + s * 1024 + 1024]
        nslab = 22
        for tc in range(NTC):
            whl = self.now(self.sS)
            for sl in range(nslab):
                wlda = self.wload(("FW", 0), "w1a", lambda e, sl=sl, l=l: e.dma_start(
                    out=W1S(0), in_=self.w1a_slab[l, sl]))
                wldg = self.wload(("FW", 1), "w1g", lambda e, sl=sl, l=l: e.dma_start(
                    out=W1S(1), in_=self.w1g_slab[l, sl]))
                for kt in range(8):
                    self.T(lambda e, sl=sl, kt=kt, tc=tc: e.matmul(
                        self.PS[0][:, 0:512], W1S(0)[:, kt * 128:kt * 128 + 128],
                        self.HLN[:, kt * 2048 + tc * 512:kt * 2048 + tc * 512 + 512],
                        start=(kt == 0), stop=(kt == 7)),
                        waits=[wlda, whl] + ([self.now(self.sS), self.now(self.sV)] if kt == 0 else []))
                self.war[("FW", 0)] = self.now(self.sT)
                for kt in range(8):
                    self.T(lambda e, sl=sl, kt=kt, tc=tc: e.matmul(
                        self.PS[1][:, 0:512], W1S(1)[:, kt * 128:kt * 128 + 128],
                        self.HLN[:, kt * 2048 + tc * 512:kt * 2048 + tc * 512 + 512],
                        start=(kt == 0), stop=(kt == 7)),
                        waits=[wldg] + ([self.now(self.sS), self.now(self.sV)] if kt == 0 else []))
                wT = self.now(self.sT)
                self.war[("FW", 1)] = wT
                self.S(lambda e, sl=sl, l=l: e.activation(
                    out=self.EXPB[:, 0:512], in_=self.PS[0][:, 0:512], func=AF.Identity,
                    bias=self.f("b1acol", 0, 128, l * 22 + sl, 1)), waits=[wT])
                self.S(lambda e, sl=sl, l=l: e.activation(
                    out=self.EXPB[:, 512:1024], in_=self.PS[1][:, 0:512], func=AF.Gelu,
                    bias=self.f("b1gcol", 0, 128, l * 22 + sl, 1)))
                wS = self.now(self.sS)
                self.V(lambda e, sl=sl: e.tensor_tensor(
                    out=self.B[:, sl * 512:sl * 512 + 512],
                    in0=self.EXPB[:, 0:512], in1=self.EXPB[:, 512:1024],
                    op=ALU.mult), waits=[wS])
            wh1 = self.now(self.sV)
            for oc in range(8):
                wldk = self.wload(("FWC", oc % 2), "w2", lambda e, oc=oc, l=l: e.dma_start(
                    out=self.Fw[:, (oc % 2) * 2816:(oc % 2) * 2816 + 2816],
                    in_=self.w2t[l, oc]),
                    extra=[self.war.get(("FW", 0)), self.war.get(("FW", 1))])
                for kt in range(22):
                    self.T(lambda e, oc=oc, kt=kt: e.matmul(
                        self.PS[2 + (oc % 2)][:, 0:512],
                        self.Fw[:, (oc % 2) * 2816 + kt * 128:(oc % 2) * 2816 + kt * 128 + 128],
                        self.B[:, kt * 512:kt * 512 + 512],
                        start=(kt == 0), stop=(kt == 21)),
                        waits=[wldk, wh1] + ([self.now(self.sV)] if kt == 0 else []))
                wT = self.now(self.sT)
                self.war[("FWC", oc % 2)] = wT
                self.war[("FW", 0)] = wT
                self.war[("FW", 1)] = wT
                self.V(lambda e, oc=oc, tc=tc, l=l: e.scalar_tensor_tensor(
                    out=self.zsl(oc, tc * 512, 512), in0=self.PS[2 + (oc % 2)][:, 0:512],
                    scalar=self.f("b2col", 0, 128, l * 8 + oc, 1),
                    in1=self.zsl(oc, tc * 512, 512), op0=ALU.add, op1=ALU.add), waits=[wT])

    def attn_sublayer(self, b):
        g = self.g; f = self.f
        Abf = self.A[:, :].bitcast(BF16)
        QT = lambda pt: Abf[:, pt * 2048:(pt + 1) * 2048]
        KT = lambda pt: Abf[:, 16384 + pt * 2048:16384 + (pt + 1) * 2048]
        YC = lambda pt: self.Fw[:, pt * 512:pt * 512 + 512]
        WSL = lambda s: self.Fw[:, 4104 + s * 1024:4104 + s * 1024 + 1024]
        XB = lambda pt: self.HLN[:, pt * 512:(pt + 1) * 512]
        SQ = lambda i: self.HLN[:, 4096 + (i % 2) * 512:4096 + (i % 2) * 512 + 512]
        TT = self.HLN[:, 5120:5632]
        WVT = self.HLN[:, 8192:16384]
        self._vwctr = getattr(self, '_vwctr', 0) + 1
        wvw = self.wload("WVT", f"attnv{self._vwctr}", lambda e: e.dma_start(
            out=WVT, in_=self.WvTr))
        for tc in range(NTC):
            wpre = self.barrierTVS() + [self.now(self.dY)]
            for pt in range(NPT):
                w = self.Vd(lambda e, pt=pt, tc=tc, b=b: e.dma_start(
                    out=XB(pt), in_=self.x_state[b][pt, :, tc * TC:(tc + 1) * TC]), waits=wpre)
            wall = w
            wmm7 = {}
            for pt in range(NPT):
                self.T(lambda e, pt=pt: e.matmul(
                    self.PS[6][0:1, 0:512], g("onescol"), XB(pt),
                    start=(pt == 0), stop=(pt == NPT - 1)),
                    waits=[wall] + ([self.now(self.sV)] if pt == 0 else []))
                self.S(lambda e, pt=pt: e.activation(out=SQ(pt), in_=XB(pt), func=AF.Square),
                       waits=[wall] + ([wmm7[pt - 2]] if pt >= 2 else []))
                wS = self.now(self.sS)
                self.T(lambda e, pt=pt: e.matmul(
                    self.PS[7][0:1, 0:512], g("onescol"), SQ(pt),
                    start=(pt == 0), stop=(pt == NPT - 1)), waits=[wS])
                wmm7[pt] = self.now(self.sT)
            wT = self.now(self.sT)
            wab = self.stats_to_ab(wT)
            TTf = self.hf(2560, 512)
            XF = lambda i: self.hf(3072 + (i % 2) * 512, 512)
            for pt in range(NPT):
                wxf = self.Xd(lambda e, pt=pt, tc=tc, b=b: e.dma_start(
                    out=XF(pt), in_=self.x_state[b][pt, :, tc * TC:(tc + 1) * TC]),
                    waits=[self.now(self.sV), self.now(self.dY)])
                self.V(lambda e, pt=pt, TTf=TTf: e.tensor_tensor(
                    out=TTf, in0=XF(pt), in1=self.PS[4][:, 0:512], op=ALU.mult),
                    waits=[wxf] + ([wab, self.now(self.sT)] if pt == 0 else []))
                self.V(lambda e, pt=pt, TTf=TTf: e.tensor_tensor(
                    out=YC(pt), in0=TTf, in1=self.PS[5][:, 0:512], op=ALU.add),
                       waits=[self.now(self.sV)])
            wy = self.now(self.sV)
            for which, Wnm, bcol, dst in ((0, "Wq", "bqcol", QT),
                                          (1, "Wk", "bkcol", KT)):
                for oc in range(8):
                    wld = self.wload(("FW", oc % 2), "attnqk", lambda e, oc=oc, Wnm=Wnm: e.dma_start(
                        out=WSL(oc % 2), in_=self.Wslab[Wnm][oc]))
                    for kt in range(8):
                        self.T(lambda e, oc=oc, kt=kt: e.matmul(
                            self.PS[oc % 2][:, 0:512],
                            WSL(oc % 2)[:, kt * 128:kt * 128 + 128], YC(kt),
                            start=(kt == 0), stop=(kt == 7)),
                            waits=[wld, wy] + ([self.now(self.sS)] if kt == 0 else []))
                    wTq = self.now(self.sT)
                    self.war[("FW", oc % 2)] = wTq
                    self.S(lambda e, oc=oc, tc=tc, bcol=bcol, dst=dst: e.activation(
                        out=dst(oc)[:, tc * 512:tc * 512 + 512],
                        in_=self.PS[oc % 2][:, 0:512], func=AF.Identity,
                        bias=self.f(bcol, 0, 128, oc, 1)), waits=[wTq])
            for ktok in range(4):
                kt_g = tc * 4 + ktok
                for half in (1, 0):
                    for kt in range(8):
                        self.T(lambda e, kt=kt, ktok=ktok, half=half: e.matmul(
                            self.PS[3 - half][:, 0:512],
                            YC(kt)[:, ktok * 128:ktok * 128 + 128],
                            WVT[:, kt * 1024 + half * 512:kt * 1024 + half * 512 + 512],
                            start=(kt == 0), stop=(kt == 7)),
                            waits=[wvw, wy] + ([self.now(self.sS), self.now(self.sV)] if kt == 0 else []))

                wTv = self.now(self.sT)
                if getattr(self, 'attn_dbg', 0) == 57 and tc == 0 and ktok == 0:
                    self.V(lambda e: e.tensor_copy(self.hf(2560, 512), self.PS[3][:, 0:512]), waits=[wTv])
                    self.V(lambda e: e.tensor_copy(self.hf(4608, 512), self.PS[2][:, 0:512]))
                    wVd = self.now(self.sV)
                    self.Yd(lambda e: e.dma_start(out=self.dbg["d1"][6][:, 0:512], in_=self.hf(2560, 512)), waits=[wVd])
                    self.Yd(lambda e: e.dma_start(out=self.dbg["d1"][7][:, 0:512], in_=self.hf(4608, 512)), waits=[wVd])
                    self.final_waits.append(self.now(self.dY))
                for half in range(2):
                    self.S(lambda e, kt_g=kt_g, half=half: e.copy(
                        self.B[:, kt_g * 1056:kt_g * 1056 + 1056]
                            .rearrange("p (h c) -> p h c", h=16)[:, half * 8:half * 8 + 8, 0:64],
                        self.PS[3 - half][:, 0:512]), waits=[wTv])
                if getattr(self, 'attn_dbg', 0) == 58 and tc == 0 and ktok == 0:
                    w58 = [self.now(self.sS)]
                    self.Pd(lambda e: e.dma_start(out=self.dbg["d1"][6][:, 0:1056], in_=self.B[:, 0:1056]), waits=w58)
                    self.Pd(lambda e: e.dma_start(out=self.dbg["d1"][5][:, 0:2048], in_=WVT[:, 0:2048]), waits=w58)
                    self.final_waits.append(self.now(self.dP))
            for h in range(NH):
                base = 64 * (h % 2)
                pti = h // 2
                tp = (64, 0) if base else None
                pg = self.PS[5] if h % 2 == 0 else self.PS[4]
                for qt in range(4):
                    self.T(lambda e, h=h, qt=qt, base=base, pti=pti, tp=tp, pg=pg: e.matmul(
                        pg[:, 2 * qt:2 * qt + 2],
                        YC(pti)[base:base + 64, qt * 128:qt * 128 + 128],
                        self.g("gw2", base, base + 64),
                        start=True, stop=True, tile_position=tp),
                        waits=[wy, self.w_gw2] +
                              ([self.now(self.sV), self.now(self.sS)] if qt == 0 else []))
                wTg = self.now(self.sT)
                self.V(lambda e, pg=pg: e.tensor_tensor(
                    out=self.hf(2560, 8), in0=pg[:, 0:8], in1=self.g("gbb", 0, 128, 0, 8),
                    op=ALU.add), waits=[wTg, self.now(self.sS)])
                wV = self.now(self.sV)
                self.S(lambda e: e.activation(out=self.hf(2568, 8), in_=self.hf(2560, 8),
                                              func=AF.Sigmoid), waits=[wV])
                wSg = self.now(self.sS)
                sg = self.hf(2568, 8).rearrange("p (a c) -> p a c", c=2)
                self.V(lambda e, h=h, sg=sg: e.tensor_scalar(
                    out=self.hf(2576, 4), in0=sg[:, :, 1],
                    scalar1=self.f("gruccol", 0, 128, h, 1), scalar2=-1.0,
                    op0=ALU.mult, op1=ALU.add), waits=[wSg])
                self.V(lambda e, sg=sg: e.tensor_tensor(
                    out=self.hf(2580, 4), in0=self.hf(2576, 4), in1=sg[:, :, 0], op=ALU.mult),
                    waits=[self.now(self.sV)])
                self.V(lambda e, h=h, tc=tc: e.tensor_scalar(
                    out=self.f("gates", 0, 128, h * 16 + tc * 4, 4), in0=self.hf(2580, 4),
                    scalar1=2.0, scalar2=None, op0=ALU.add), waits=[self.now(self.sV)])
        wS = self.now(self.sS)
        self.V(lambda e: e.memset(
            self.B[:, 0:16 * 1056].rearrange("p (x c) -> p x c", c=66)[:, :, 64:65],
            1.0), waits=[wS])
        wvones = self.now(self.sV)
        wgates = wvones
        if getattr(self, 'attn_dbg', 0) == 51:
            for pt in range(NPT):
                self.Pd(lambda e, pt=pt: e.dma_start(
                    out=self.dbg["d1"][pt], in_=QT(pt)), waits=[wvones])
            self.final_waits.append(self.now(self.dP))
            return
        if getattr(self, 'attn_dbg', 0) == 52:
            for pt in range(NPT):
                self.Pd(lambda e, pt=pt: e.dma_start(
                    out=self.dbg["d1"][pt], in_=KT(pt)), waits=[wvones])
            self.final_waits.append(self.now(self.dP))
            return
        if getattr(self, 'attn_dbg', 0) == 53:
            self.attn_dump53 = True

        for pti in range(8):
            i0 = 1024 + pti * 16
            for which, srcf in ((0, QT), (1, KT)):
                self.S(lambda e, srcf=srcf, pti=pti: e.activation(
                    out=self.HLN[:, 8192:10240], in_=srcf(pti), func=AF.Square),
                    waits=[self.now(self.sV), self.now(self.sT)])
                wS = self.now(self.sS)
                for tc in range(NTC):
                    self.T(lambda e, tc=tc: e.matmul(
                        self.PS[7][0:1, 0:512], g("onescol"),
                        self.HLN[:, 8192 + tc * 512:8192 + tc * 512 + 512],
                        start=True, stop=True),
                        waits=([wS] if tc == 0 else []) + [self.now(self.sV)])
                    wT = self.now(self.sT)
                    self.V(lambda e, tc=tc, which=which, i0=i0: e.tensor_reduce(
                        self.R1[0:1, i0 + which * 8 + tc:i0 + which * 8 + tc + 1],
                        self.PS[7][0:1, 0:512], axis=AX.X, op=ALU.max), waits=[wT])
                self.V(lambda e, which=which, i0=i0: e.tensor_reduce(
                    self.R1[0:1, i0 + which * 8 + 4:i0 + which * 8 + 5],
                    self.R1[0:1, i0 + which * 8:i0 + which * 8 + 4],
                    axis=AX.X, op=ALU.max), waits=[self.now(self.sV)])
            wV = self.now(self.sV)
            self.S(lambda e, i0=i0: e.sqrt(self.R1[0:1, i0 + 5:i0 + 6], self.R1[0:1, i0 + 4:i0 + 5]),
                   waits=[wV])
            self.S(lambda e, i0=i0: e.sqrt(self.R1[0:1, i0 + 13:i0 + 14], self.R1[0:1, i0 + 12:i0 + 13]))
            wS = self.now(self.sS)
            self.V(lambda e, i0=i0: e.tensor_tensor(
                out=self.R1[0:1, i0 + 6:i0 + 7], in0=self.R1[0:1, i0 + 5:i0 + 6],
                in1=self.R1[0:1, i0 + 13:i0 + 14], op=ALU.mult), waits=[wS])
            self.V(lambda e, i0=i0: e.tensor_scalar(
                out=self.R1[0:1, i0 + 7:i0 + 8], in0=self.R1[0:1, i0 + 6:i0 + 7],
                scalar1=0.125, scalar2=None, op0=ALU.mult), waits=[self.now(self.sV)])
            self.V(lambda e, i0=i0: e.tensor_tensor(
                out=self.R1[0:1, i0 + 14:i0 + 15], in0=self.R1[0:1, i0 + 7:i0 + 8],
                in1=self.SC[0:1, 6:7], op=ALU.add),
                   waits=[self.w_bterm, self.now(self.sV)])
            self.V(lambda e, i0=i0, pti=pti: e.tensor_scalar(
                out=self.R1b[0:1, 50 + pti:51 + pti], in0=self.R1[0:1, i0 + 14:i0 + 15],
                scalar1=-1.0, scalar2=None, op0=ALU.mult), waits=[self.now(self.sV)])
            wV2 = self.now(self.sV)
            self.T(lambda e, pti=pti: e.matmul(self.PS[4][:, 20:21], self.g("ones1", 0, 1),
                                               self.R1b[0:1, 50 + pti:51 + pti],
                                               start=True, stop=True),
                   waits=[wV2, self.now(self.sS)])
            wT = self.now(self.sT)
            self.S(lambda e, pti=pti: e.copy(self.f("signeg", 0, 128, 2 + pti, 1),
                                             self.PS[4][:, 20:21]), waits=[wT])

        if getattr(self, 'attn_dbg', 0) == 56:
            wpre = [self.now(self.sS), self.now(self.sV), self.now(self.sT), self.now(self.dP)]
            self.Pd(lambda e: e.dma_start(out=self.dbg["d1"][6][:, 0:1056], in_=self.B[:, 0:1056]), waits=wpre)
            self.Pd(lambda e: e.dma_start(out=self.dbg["d1"][5][:, 0:2048], in_=WVT[:, 0:2048]), waits=wpre)
            self.final_waits.append(self.now(self.dP))
            return
        if getattr(self, 'attn_dump53', False):
            wpre = [self.now(self.sS), self.now(self.sV), self.now(self.sT)]
            self.Yd(lambda e: e.dma_start(
                out=self.dbg["d1"][0][:, 0:256], in_=self.f("gates")), waits=wpre)
            self.Yd(lambda e: e.dma_start(
                out=self.dbg["d1"][0][:, 256:268], in_=self.f("signeg")), waits=wpre)
            self.final_waits.append(self.now(self.dY))
            return
        GATED = lambda qt: self.C[:, 4096 + qt * 2048:4096 + qt * 2048 + 2048]
        STRIP = self.C[:, 12288:12288 + 3968]
        CTX = lambda pt: self.C[:, pt * 512:(pt + 1) * 512]
        RB = self.HLN[0:64, 8192:8704]
        for qc in range(NTC):
            for h in range(NH):
                base = 64 * (h % 2)
                pti = h // 2
                tp = (64, 0) if base else None
                widx = self.sch.op("gpsimd", lambda e, h=h: e.dma_start(
                    out=self.HLN[:, 10240:10242].bitcast(U32),
                    in_=self.c_stripidx[h]), (self.now(self.sT),), ((self.dPi, 16),))
                wstrip = self.sch.op("gpsimd", lambda e: e.indirect_dma_start(
                    out=STRIP, out_offset=None, in_=self.dv_dram,
                    in_offset=IndirectOffsetOnAxis(
                        ap=self.HLN[:, 10240:10242].bitcast(U32), axis=0)),
                    (widx, self.w_dv, self.now(self.sV), self.now(self.sT)),
                    ((self.dPs, 16),))
                wg = []
                for qt in range(4):
                    qtg = qc * 4 + qt
                    w = self.V(lambda e, qt=qt, qtg=qtg, h=h: e.tensor_scalar(
                        out=GATED(qt),
                        in0=STRIP[:, (15 - qtg) * 128:(15 - qtg) * 128 + 2048],
                        scalar1=self.f("gates", 0, 128, h * 16 + qtg, 1),
                        scalar2=QK_SCALE, op0=ALU.mult, op1=ALU.mult),
                        waits=[wstrip, wgates, self.now(self.sT)])
                    wg.append(w)
                sneg = self.f("signeg", 0, 128, 2 + pti, 1)
                wexp_prev = {}
                for pr in range(8):
                    prb = pr % 2
                    PT = self.PSW if prb == 0 else self.PSX
                    base_ps = 0 if prb == 0 else 6
                    wW = ([wexp_prev[prb]] if prb in wexp_prev else [self.now(self.sS)]) + wg
                    for half in range(2):
                        kt = 2 * pr + half
                        ksl = KT(pti)[base:base + 64, kt * 128:kt * 128 + 128]
                        qsl = QT(pti)[base:base + 64, qc * 512:qc * 512 + 512]
                        self.T(lambda e, ksl=ksl, qsl=qsl, base_ps=base_ps, half=half, tp=tp: e.matmul(
                            self.PS[base_ps + half][:, 0:512], ksl, qsl,
                            start=True, stop=False, tile_position=tp),
                            waits=wW if half == 0 else ())
                        for qt in range(4):
                            self.T(lambda e, base_ps=base_ps, half=half, qt=qt, kt=kt, st=(qt == 3): e.matmul(
                                self.PS[base_ps + half][:, qt * 128:qt * 128 + 128],
                                GATED(qt)[:, kt * 128:kt * 128 + 128], g("ident"),
                                start=False, stop=st))
                    wT = self.now(self.sT)
                    self.S(lambda e, PT=PT, prb=prb, sneg=sneg: e.activation(
                        out=self.EXPB[:, prb * 1024:prb * 1024 + 1024],
                        in_=PT[:, 0:1024], func=AF.Exp,
                        bias=sneg, scale=0.125), waits=[wT])
                    wexp = self.now(self.sS)
                    wexp_prev[prb] = wexp
                    for half in range(2):
                        kt = 2 * pr + half
                        self.T(lambda e, kt=kt, h=h, prb=prb, half=half: e.matmul(
                            self.PS[2][0:65, 0:512],
                            self.B[:, kt * 1056 + h * 66:kt * 1056 + h * 66 + 65],
                            self.EXPB[:, prb * 1024 + half * 512:prb * 1024 + half * 512 + 512],
                            start=(kt == 0), stop=(kt == 15)),
                            waits=[wexp, wvones] + ([self.now(self.sV)] if kt == 0 else []))
                wTpv = self.now(self.sT)
                self.V(lambda e: e.tensor_scalar(
                    out=self.R2[0:1, 1024:1536], in0=self.PS[2][64:65, 0:512],
                    scalar1=1e-30, scalar2=None, op0=ALU.add), waits=[wTpv])
                wVd_ = self.now(self.sV)
                self.S(lambda e: e.activation(out=self.R2[0:1, 0:512], in_=self.R2[0:1, 1024:1536],
                                              func=AF.Ln), waits=[wVd_])
                self.S(lambda e: e.activation(out=self.R2b[0:1, 0:512], in_=self.R2[0:1, 0:512],
                                              func=AF.Exp, scale=-1.0), waits=[self.now(self.sS)])
                wV = self.now(self.sS)
                self.T(lambda e: e.matmul(self.PS[4][0:64, 0:512], g("pat2", 0, 2)[0:1, 0:64],
                                          self.R2b[0:1, 0:512], start=True, stop=True),
                       waits=[wV, self.now(self.sS)])
                wT2 = self.now(self.sT)
                self.S(lambda e: e.copy(RB, self.PS[4][0:64, 0:512]), waits=[wT2])
                wS3 = self.now(self.sS)
                if h % 2 == 0:
                    self.V(lambda e, pti=pti: e.tensor_tensor(
                        out=CTX(pti)[0:64, :], in0=self.PS[2][0:64, 0:512], in1=RB,
                        op=ALU.mult), waits=[wS3, self.now(self.sT)])
                else:
                    self.V(lambda e: e.tensor_tensor(
                        out=self.ODD[0:64, 0:512], in0=self.PS[2][0:64, 0:512], in1=RB,
                        op=ALU.mult), waits=[wS3, self.now(self.dPo), self.now(self.sT)])
                    wV2 = self.now(self.sV)
                    self.sch.op("gpsimd", lambda e, pti=pti: e.dma_start(
                        out=CTX(pti)[64:128, :], in_=self.ODD[0:64, 0:512]), (wV2,), ((self.dPo, 16),))
            if getattr(self, 'attn_dbg', 0) == 55 and qc == 0:
                wpre = [self.now(self.sS), self.now(self.sV), self.now(self.sT), self.now(self.dP)]
                self.Pd(lambda e: e.dma_start(out=self.dbg["d1"][6][:, 0:1056], in_=self.B[:, 0:1056]), waits=wpre)
                self.Pd(lambda e: e.dma_start(out=self.dbg["d1"][7][:, 0:2048], in_=self.Abf[:, 0:2048] if hasattr(self, 'Abf') else self.A[:, :].bitcast(BF16)[:, 0:2048]), waits=wpre)
                self.Pd(lambda e: e.dma_start(out=self.dbg["d1"][0][:, 0:2048], in_=STRIP[:, 0:2048]), waits=wpre)
                self.Pd(lambda e: e.dma_start(out=self.dbg["d1"][4][:, 0:1920], in_=STRIP[:, 2048:3968]), waits=wpre)
                self.Pd(lambda e: e.dma_start(out=self.dbg["d1"][1][:, 0:2048], in_=GATED(0)), waits=wpre)
                self.Pd(lambda e: e.dma_start(out=self.dbg["d1"][2][:, 0:2048], in_=self.EXPB[:, :]), waits=wpre)
                self.Pd(lambda e: e.dma_start(out=self.dbg["d1"][3][:, 0:2048], in_=self.C[:, 0:2048]), waits=wpre)
                self.Pd(lambda e: e.dma_start(out=self.dbg["d1"][5][:, 0:2048], in_=self.C[:, 2048:4096]), waits=wpre)
                self.final_waits.append(self.now(self.dP))
                return
            wodd = self.now(self.dPo)
            wctx = self.now(self.sV)
            XC = self.hf(5632, 512)
            RC = self.hf(6144, 512)
            for oc in range(8):
                wld = self.wload(("FW", oc % 2), "attnout", lambda e, oc=oc: e.dma_start(
                    out=WSL(oc % 2), in_=self.Wslab["Wo"][oc]))
                for kt in range(8):
                    self.T(lambda e, oc=oc, kt=kt: e.matmul(
                        self.PS[5 + (oc % 2)][:, 0:512],
                        WSL(oc % 2)[:, kt * 128:kt * 128 + 128],
                        CTX(kt), start=(kt == 0), stop=(kt == 7)),
                        waits=[wld, wodd, wctx] + ([self.now(self.sV)] if kt == 0 else []))
                wT = self.now(self.sT)
                self.war[("FW", oc % 2)] = wT
                wx = self.Xd(lambda e, oc=oc, qc=qc, b=b: e.dma_start(
                    out=XC, in_=self.x_state[b][oc, :, qc * 512:qc * 512 + 512]),
                    waits=[self.now(self.sV)])
                self.V(lambda e, oc=oc: e.scalar_tensor_tensor(
                    out=RC, in0=self.PS[5 + (oc % 2)][:, 0:512],
                    scalar=self.f("bo2col", 0, 128, oc, 1),
                    in1=XC, op0=ALU.add, op1=ALU.add), waits=[wT, wx])
                wV = self.now(self.sV)
                self.Yd(lambda e, oc=oc, qc=qc, b=b: e.dma_start(
                    out=self.x_state[b][oc, :, qc * 512:qc * 512 + 512], in_=RC),
                    waits=[wV])
        self.final_waits.append(self.now(self.dY))


def build_program(debug_stage=0):
    nc = bass.Bass("TRN2", target_bir_lowering=False)
    bd = BD(nc, debug_stage)
    ctx = ExitStack()
    with ctx:
        bd.alloc(ctx)
        bd.mk_sems()
        bd.body()
        for w in bd.final_waits:
            bd.sch.op("sync", None, [w], ())
        blk = ctx.enter_context(nc.Block())
        bd.sch.emit(blk)
    return nc


def kernel(**inputs):
    d = prep_host(inputs)
    nc = build_program(DEBUG_STAGE)
    xin_all = d.pop("xin")
    in_maps = []
    for core in range(8):
        m = dict(d)
        m["xin"] = np.ascontiguousarray(xin_all[2 * core:2 * core + 2])
        in_maps.append(m)
    res = run_bass_kernel_spmd(nc, in_maps, core_ids=list(range(8)))
    outs = [np.asarray(res.results[c]["xout"]).reshape(2, E, N) for c in range(8)]
    full = np.concatenate(outs, axis=0)
    return np.ascontiguousarray(np.transpose(full, (0, 2, 1)))
